# version 13
# baseline (speedup 1.0000x reference)
import numpy as np
from contextlib import ExitStack
import concourse.bass as bass
import concourse.mybir as mybir
from concourse.bass_utils import run_bass_kernel_spmd

F32 = mybir.dt.float32
BF16 = mybir.dt.bfloat16
I16 = mybir.dt.int16
U32 = mybir.dt.uint32
AF = mybir.ActivationFunctionType
ALU = mybir.AluOpType
AX = mybir.AxisListType

T, NT = 2048, 16
CAP = 1024
D, ND = 1024, 8
DFF, NDF = 2048, 16
TH, NTH = 1024, 2
EPS = 1e-5

ENGINES = ("tensor", "vector", "scalar", "gpsimd", "sync")
ESEM = {"tensor": "T", "vector": "V", "scalar": "S", "gpsimd": "G", "sync": "Y"}


class Bld:

    NWD = 12
    NIO = 4

    def __init__(self):
        self.ev = {e: [] for e in ENGINES}
        self.cnt = {}
        self.lastw = {e: {} for e in ENGINES}
        self.sems = {}
        self.wd_rr = 0
        self.io_rr = 0

    def emit(self, eng, mk, after=(), guard=(), dma=False, inc=True):
        lane_prev = None
        if dma:
            if eng == "gpsimd":
                sem = f"WD{self.wd_rr % self.NWD}"
                self.wd_rr += 1
            else:
                sem = f"IO{self.io_rr % self.NIO}"
                self.io_rr += 1
            if self.cnt.get(sem, 0) > 0:
                lane_prev = (sem, self.cnt[sem])
        else:
            sem = ESEM[eng]
        want = {}
        if lane_prev is not None:
            want[lane_prev[0]] = lane_prev[1]
        for t in after:
            if t is None:
                continue
            s, v = t
            if want.get(s, 0) < v:
                want[s] = v
        for t in guard:
            if t is None:
                continue
            s, v = t
            if want.get(s, 0) < v:
                want[s] = v
        lw = self.lastw[eng]
        ws = []
        for s, v in want.items():
            if lw.get(s, 0) >= v:
                continue
            lw[s] = v
            ws.append((s, v))
        tick = None
        n = 16 if dma else 1
        if inc:
            self.cnt[sem] = self.cnt.get(sem, 0) + n
            tick = (sem, self.cnt[sem])
        B = self

        def fn(e, _ws=tuple(ws), _mk=mk, _sem=sem, _n=n, _inc=inc):
            for s, v in _ws:
                e.wait_ge(B.sems[s], v)
            ins = _mk(e)
            if _inc:
                ins.then_inc(B.sems[_sem], _n)
        self.ev[eng].append(fn)
        return tick


DEBUG = False
N_ROUTED = 8


def build_program():
    nc = bass.Bass("TRN2", target_bir_lowering=False)
    b = Bld()

    def tap(name, ap, after, shape, dtype=F32):
        if not DEBUG:
            return
        d = nc.dram_tensor("dbg_" + name, shape, dtype, kind="ExternalOutput").ap()
        b.emit("sync", lambda e: e.dma_start(out=d[:, :], in_=ap), dma=True,
               after=list(after))
    bank_last = [None] * 8

    dx = nc.dram_tensor("x", [T, D], F32, kind="ExternalInput").ap()
    dy = nc.dram_tensor("y", [T, D], F32, kind="ExternalOutput").ap()
    dw = {}
    for nm in ("wq", "wk", "wv", "wqa", "wka"):
        dw[nm] = nc.dram_tensor(nm, [D, D], F32, kind="ExternalInput").ap()
    for nm in ("bq", "bk", "bv", "bqa", "bka"):
        dw[nm] = nc.dram_tensor(nm, [1, D], F32, kind="ExternalInput").ap()
    dw["gcol"] = nc.dram_tensor("gcol", [128, ND], F32, kind="ExternalInput").ap()
    dw["bcol"] = nc.dram_tensor("bcol", [128, ND], F32, kind="ExternalInput").ap()
    dw["gatew"] = nc.dram_tensor("gatew", [D, 8], F32, kind="ExternalInput").ap()
    lnhbm = nc.dram_tensor("lnhbm", [T, D], BF16, kind="Internal").ap()
    whbm = nc.dram_tensor("whbm", [T, 64], F32, kind="Internal").ap()
    for e in range(9):
        dw[f"wg{e}"] = nc.dram_tensor(f"wg{e}", [D, DFF], F32, kind="ExternalInput").ap()
        dw[f"wu{e}"] = nc.dram_tensor(f"wu{e}", [D, DFF], F32, kind="ExternalInput").ap()
        dw[f"wd{e}"] = nc.dram_tensor(f"wd{e}", [DFF, D], F32, kind="ExternalInput").ap()

    es = ExitStack()

    def sb(name, shape, dt):
        return es.enter_context(nc.sbuf_tensor(name, shape, dt))

    with es:
        xres = sb("xres", [128, NT * D], F32)
        Ft = sb("Ft", [128, 128], F32)
        Pc = sb("Pc", [128, 1], F32)
        identb = sb("identb", [128, 128], BF16)
        u128b = sb("u128b", [128, 128], BF16)
        ones128b = sb("ones128b", [128, 128], BF16)
        onescol = sb("onescol", [1, 128], BF16)
        tvals = sb("tvals", [128, NT], F32)
        rt = sb("rt", [128, NT], F32)
        gcolv = sb("gcolv", [128, ND], F32)
        bcolv = sb("bcolv", [128, ND], F32)
        bcolb = sb("bcolb", [128, ND], BF16)
        sumc = sb("sumc", [128, NT], F32)
        sqc = sb("sqc", [128, NT], F32)
        muc = sb("muc", [128, NT], F32)
        rsc = sb("rsc", [128, NT], F32)
        mu2c = sb("mu2c", [128, NT], F32)
        rs2c = sb("rs2c", [128, NT], F32)
        vp16 = sb("vp16", [128, NT], F32)
        nm1c = sb("nm1c", [128, 1], F32)
        nm2c = sb("nm2c", [128, 1], F32)
        edc = sb("edc", [128, 1], F32)
        selc = sb("selc", [128, 8], F32)
        nmr = sb("nmr", [128, 1], F32)
        ssr = sb("ssr", [128, 1], F32)
        srr = sb("srr", [128, 1], F32)

        rows_g = {nm: nc.sbuf_tensor("r_" + nm, [1, D], BF16)
                  for nm in ("bq", "bk", "bv", "bqa", "bka")}
        rows = {nm: g.__enter__() for nm, g in rows_g.items()}
        tmps_g = nc.sbuf_tensor("tmps", [128, D], BF16)
        tmps = tmps_g.__enter__()
        awb_g = {nm: nc.sbuf_tensor("b_" + nm, [128, ND * D], BF16) for nm in
                 ("wq", "wk", "wv", "wqa", "wka")}
        awb = {nm: g.__enter__() for nm, g in awb_g.items()}
        rot_g = {}
        rot = {}
        for nm in ("nb", "nTb", "qb", "qb1", "qTb", "kb", "kb1", "vb", "vb1",
                   "mkb", "mkTb", "pqb", "pkb", "Ac", "Bc", "qwb", "S1", "S2"):
            rot_g[nm] = nc.sbuf_tensor("t_" + nm, [128, D], BF16)
            rot[nm] = rot_g[nm].__enter__()
        qas_g = nc.sbuf_tensor("qas", [128, 2 * D], F32)
        qas = qas_g.__enter__()
        qbufs = (rot["qb"], rot["qb1"])
        kbufs = (rot["kb"], rot["kb1"])
        vbufs = (rot["vb"], rot["vb1"])

        pb = [es.enter_context(nc.psum_tensor(f"pb{i}", [128, 512], F32)) for i in range(8)]

        for s in (["T", "V", "S", "G", "Y"]
                  + [f"WD{i}" for i in range(Bld.NWD)]
                  + [f"IO{i}" for i in range(Bld.NIO)]):
            b.sems[s] = es.enter_context(nc.semaphore("sem" + s))

        t_x = b.emit("sync", lambda e: e.dma_start(
            out=xres[:], in_=dx.rearrange("(c p) f -> p c f", p=128)), dma=True)
        io_rows = {}
        for nm in ("bq", "bk", "bv", "bqa", "bka"):
            io_rows[nm] = b.emit("gpsimd", lambda e, a=rows[nm], d=dw[nm]: e.dma_start(
                out=a[:], in_=d[:, :]), dma=True)
        t_gcol = b.emit("sync", lambda e: e.dma_start(out=gcolv[:], in_=dw["gcol"][:, :]), dma=True)
        t_bcol = b.emit("sync", lambda e: e.dma_start(out=bcolv[:], in_=dw["bcol"][:, :]), dma=True)

        t_iF = b.emit("gpsimd", lambda e: e.iota(
            Ft[:], [[1, 128]], base=0, channel_multiplier=0,
            allow_small_or_imprecise_dtypes=True))
        t_iP = b.emit("gpsimd", lambda e: e.iota(
            Pc[:], [[0, 1]], base=0, channel_multiplier=1,
            allow_small_or_imprecise_dtypes=True))
        t_itv = b.emit("gpsimd", lambda e: e.iota(
            tvals[:], [[128, NT]], base=1, channel_multiplier=1,
            allow_small_or_imprecise_dtypes=True))
        t_ones = b.emit("gpsimd", lambda e: e.memset(ones128b[:], 1.0))
        t_onescol = b.emit("gpsimd", lambda e: e.memset(onescol[:], 1.0))
        t_s1 = b.emit("gpsimd", lambda e: e.memset(rot["S1"][:], 0.0))
        t_s2 = b.emit("gpsimd", lambda e: e.memset(rot["S2"][:], 0.0))

        t_ident = b.emit("vector", lambda e: e.tensor_scalar(
            out=identb[:], in0=Ft[:], scalar1=Pc[:, 0:1], scalar2=None,
            op0=ALU.is_equal), after=[t_iF, t_iP])
        t_u128 = b.emit("vector", lambda e: e.tensor_scalar(
            out=u128b[:], in0=Ft[:], scalar1=Pc[:, 0:1], scalar2=None,
            op0=ALU.is_ge), after=[t_iF, t_iP])
        t_rt = b.emit("vector", lambda e: e.reciprocal(rt[:], tvals[:]), after=[t_itv])
        t_bcolb = b.emit("vector", lambda e: e.tensor_copy(bcolb[:], bcolv[:]), after=[t_bcol])

        t_aw = {}
        for nm in ("wq", "wk", "wv", "wqa", "wka"):
            t_aw[nm] = b.emit("gpsimd", lambda e, a=awb[nm], d=dw[nm]: e.dma_start(
                out=a[:], in_=d.rearrange("(c p) f -> p c f", p=128)), dma=True)

        def mm_group(bank_i, emits, extra_after=(), evac=None, evac_eng="scalar",
                     evac_guard=()):
            bank = pb[bank_i]
            n = len(emits)
            gt = None
            for i, mk in enumerate(emits):
                aft = (list(extra_after) + [bank_last[bank_i]]) if i == 0 else ()
                gt = b.emit("tensor",
                            lambda e, _mk=mk, _bk=bank, _st=(i == 0), _sp=(i == n - 1): _mk(
                                e, _bk, _st, _sp),
                            after=aft, inc=(i == n - 1))
            ev = None
            if evac is not None:
                ev = b.emit(evac_eng, lambda e, _mk=evac, _bk=bank: _mk(e, _bk),
                            after=[gt], guard=evac_guard)
                bank_last[bank_i] = ev
            else:
                bank_last[bank_i] = gt
            return gt, ev

        t_br = {}
        for i, nm in enumerate(("bq", "bk", "bv")):
            wnm = ("wq", "wk", "wv")[i]
            ev = None
            for s in range(2):
                emits = []
                for dc in range(ND):
                    emits.append(lambda e, bk, st, sp, _dc=dc, _s=s, _w=awb[wnm]: e.matmul(
                        bk[0:1, 0:512], bcolb[:, _dc:_dc + 1],
                        _w[:, _dc * D + _s * 512:_dc * D + _s * 512 + 512],
                        start=st, stop=False))
                emits.append(lambda e, bk, st, sp, _s=s, _r=rows[nm]: e.matmul(
                    bk[0:1, 0:512], onescol[0:1, 0:1], _r[0:1, _s * 512:_s * 512 + 512],
                    start=False, stop=True))
                _, ev = mm_group(6, emits,
                                 extra_after=[t_aw[wnm], t_bcolb, t_onescol, io_rows[nm]],
                                 evac=lambda e, bk, _s=s, _o=rows[nm]: e.copy(
                                     _o[0:1, _s * 512:_s * 512 + 512], bk[0:1, 0:512]))
            t_br[nm] = ev

        t_fold = {}
        for i, wnm in enumerate(("wq", "wk", "wv")):
            last = None
            for dc in range(ND):
                last = b.emit("vector", lambda e, _dc=dc, _w=awb[wnm]: e.tensor_scalar(
                    out=_w[:, _dc * D:(_dc + 1) * D], in0=_w[:, _dc * D:(_dc + 1) * D],
                    scalar1=gcolv[:, _dc:_dc + 1], scalar2=None, op0=ALU.mult),
                    after=[t_br[("bq", "bk", "bv")[i]], t_gcol, t_aw[wnm]])
            t_fold[wnm] = last

        tap("brow_bq", rows["bq"][:], [t_br["bq"]], [1, D], BF16)
        tap("wq_fold", awb["wq"][:, 0:2048], [t_fold["wq"]], [128, 2048], BF16)
        def transpose8(src_bf, dst_bf, after, banks=(6, 7)):
            evt = []
            for half in range(2):
                emits = []
                for j in range(4):
                    emits.append(lambda e, bk, st, sp, _j=j, _dc=half * 4 + j: e.matmul(
                        bk[:, _j * 128:(_j + 1) * 128],
                        src_bf[:, _dc * 128:(_dc + 1) * 128], identb[:],
                        start=st, stop=sp))
                _, ev = mm_group(banks[half], emits,
                                 extra_after=list(after) + [t_ident],
                                 evac=lambda e, bk, _h=half: e.copy(
                                     dst_bf[:, _h * 512:(_h + 1) * 512], bk[:, 0:512]))
                evt.append(ev)
            return evt

        def proj(dst_bf, srcT, wname, brow_t, brow_ap, after, banks=(0, 1), scale=None,
                 dst_f32=None):
            evt = []
            for s in range(2):
                emits = []
                for dc in range(ND):
                    emits.append(lambda e, bk, st, sp, _dc=dc, _s=s: e.matmul(
                        bk[:], srcT[:, _dc * 128:(_dc + 1) * 128],
                        awb[wname][:, _dc * D + _s * 512:_dc * D + _s * 512 + 512],
                        start=st, stop=False))
                emits.append(lambda e, bk, st, sp, _s=s: e.matmul(
                    bk[:], onescol[0:1, :], brow_ap[0:1, _s * 512:_s * 512 + 512],
                    start=False, stop=True))
                if scale is None:
                    evac = lambda e, bk, _s=s: e.copy(
                        dst_bf[:, _s * 512:(_s + 1) * 512], bk[:])
                else:
                    evac = lambda e, bk, _s=s: e.mul(
                        dst_f32[:, _s * 512:(_s + 1) * 512], bk[:], scale)
                _, ev = mm_group(banks[s], emits,
                                 extra_after=list(after) + [t_aw[wname], brow_t, t_onescol],
                                 evac=evac)
                evt.append(ev)
            return evt

        pi = {}
        x2_ticks = []
        t_sum_last = t_sq_last = None
        for c in range(NT):
            xc0 = xres[:, c * D:(c + 1) * D]
            t_sum_last = b.emit("vector", lambda e, _c=c, _x=xc0: e.tensor_reduce(
                out=sumc[:, _c:_c + 1], in_=_x, axis=AX.X, op=ALU.add), after=[t_x])
            t_sq_last = b.emit("scalar", lambda e, _c=c, _x=xc0: e.activation(
                out=tmps[:], in_=_x, func=AF.Square, bias=0.0, scale=1.0,
                accum_out=sqc[:, _c:_c + 1]), after=[t_x], guard=[t_sq_last])
        t_mu = b.emit("vector", lambda e: e.tensor_scalar(
            out=muc[:], in0=sumc[:], scalar1=1.0 / D, scalar2=None, op0=ALU.mult),
            guard=[t_sum_last])
        t_msq = b.emit("vector", lambda e: e.tensor_tensor(
            out=vp16[:], in0=muc[:], in1=muc[:], op=ALU.mult), guard=[t_mu])
        t_va = b.emit("vector", lambda e: e.scalar_tensor_tensor(
            out=vp16[:], in0=sqc[:], scalar=1.0 / D, in1=vp16[:],
            op0=ALU.mult, op1=ALU.subtract), after=[t_sq_last], guard=[t_msq])
        t_vp = b.emit("vector", lambda e: e.tensor_scalar(
            out=vp16[:], in0=vp16[:], scalar1=1.0 + EPS, scalar2=EPS * EPS,
            op0=ALU.mult, op1=ALU.add), guard=[t_va])
        t_ln = b.emit("scalar", lambda e: e.activation(
            out=rsc[:], in_=vp16[:], func=AF.Ln, bias=0.0, scale=1.0), after=[t_vp])
        t_rs = b.emit("scalar", lambda e: e.activation(
            out=rsc[:], in_=rsc[:], func=AF.Exp, bias=0.0, scale=-0.5), guard=[t_ln])
        front = [None] * NT
        backs = {}

        nticks = {}

        def F1V(c):
            xc = xres[:, c * D:(c + 1) * D]
            nticks[c] = b.emit("vector", lambda e, _c=c, _x=xc: e.tensor_scalar(
                out=rot["nb"][:], in0=_x, scalar1=muc[:, _c:_c + 1],
                scalar2=rsc[:, _c:_c + 1], op0=ALU.subtract, op1=ALU.mult),
                after=[t_rs, pi.get("nb")], guard=[t_mu])

        def F1T(c):
            r = c % 2
            t_n = nticks[c]
            t_nT = transpose8(rot["nb"], rot["nTb"], [t_n, pi.get("nTb")])
            t_q = proj(qbufs[r], rot["nTb"], "wq", t_br["bq"], rows["bq"],
                       t_nT + [t_fold["wq"], pi.get(("qb", r))], banks=(0, 1))
            t_k = proj(kbufs[r], rot["nTb"], "wk", t_br["bk"], rows["bk"],
                       t_nT + [t_fold["wk"], pi.get(("kb", r))], banks=(2, 3))
            t_v = proj(vbufs[r], rot["nTb"], "wv", t_br["bv"], rows["bv"],
                       t_nT + [t_fold["wv"], pi.get(("vb", r))], banks=(4, 5))
            pi["nb"] = t_nT[1]
            pi["nTb"] = t_v[1]
            front[c] = [t_q, t_k, t_v, None]
            if c == 0:
                tap("nb", rot["nb"][:], [t_n], [128, D], BF16)
                tap("nTb", rot["nTb"][:], t_nT, [128, D], BF16)
                tap("qb", qbufs[0][:], t_q, [128, D], BF16)
                tap("kb", kbufs[0][:], t_k, [128, D], BF16)
                tap("vb", vbufs[0][:], t_v, [128, D], BF16)

        def F2(c):
            r = c % 2
            qasb = qas[:, r * D:(r + 1) * D]
            t_q = front[c][0]
            t_qT = transpose8(qbufs[r], rot["qTb"], t_q + [pi.get("qTb")])
            t_qa = proj(None, rot["qTb"], "wqa", io_rows["bqa"], rows["bqa"],
                        t_qT + [pi.get(("qas", r))], banks=(0, 1),
                        scale=1.0 / 32.0, dst_f32=qasb)
            pi["qTb"] = t_qa[1]
            front[c][3] = t_qa
            if c == 0:
                tap("qas", qas[:, 0:D], t_qa, [128, D])

        def B1(c):
            r = c % 2
            qasb = qas[:, r * D:(r + 1) * D]
            t_qa = front[c][3]
            t_nm = b.emit("vector", lambda e, _q=qasb: e.tensor_reduce(
                out=nmr[:], in_=_q, axis=AX.X, op=ALU.max, negate=True),
                after=t_qa + [pi.get("nmr")])
            t_qw = b.emit("scalar", lambda e, _q=qasb: e.activation(
                out=rot["qwb"][:], in_=_q, func=AF.Exp, bias=nmr[:, 0:1], scale=1.0,
                accum_out=ssr[:, 0:1]), after=[t_nm, pi.get("qwb")])
            t_sr = b.emit("vector", lambda e: e.reciprocal(srr[:], ssr[:]), after=[t_qw])
            t_A = b.emit("vector", lambda e, _r=r: e.scalar_tensor_tensor(
                out=rot["Ac"][:], in0=rot["qwb"][:], scalar=srr[:, 0:1], in1=qbufs[_r][:],
                op0=ALU.mult, op1=ALU.mult), after=[pi.get("Ac")], guard=[t_sr])
            pi[("qb", r)] = t_A
            backs["qw"] = t_qw
            backs["A"] = t_A
            if c == 0:
                tap("qwb", rot["qwb"][:], [t_qw], [128, D], BF16)
                tap("Ac", rot["Ac"][:], [t_A], [128, D], BF16)

        def B2(c):
            r = c % 2
            qasb = qas[:, r * D:(r + 1) * D]
            t_q, t_k, t_v, t_qa = front[c]
            t_A = backs["A"]
            t_cs1 = []
            for s in range(2):
                emits = [
                    lambda e, bk, st, sp, _s=s: e.matmul(
                        bk[:], u128b[:], rot["Ac"][:, _s * 512:(_s + 1) * 512],
                        start=True, stop=False),
                    lambda e, bk, st, sp, _s=s: e.matmul(
                        bk[:], ones128b[:], rot["S1"][:, _s * 512:(_s + 1) * 512],
                        start=False, stop=True),
                ]
                _, ev = mm_group((0, 1)[s], emits,
                                 extra_after=[t_A, t_u128, t_ones, t_s1, pi.get("S1"),
                                              t_rt],
                                 evac=lambda e, bk, _s=s, _c=c: e.activation(
                                     out=rot["pqb"][:, _s * 512:(_s + 1) * 512], in_=bk[:],
                                     func=AF.Copy, bias=0.0, scale=rt[:, _c:_c + 1]))
                t_cs1.append(ev)
            t_S1 = b.emit("vector", lambda e: e.tensor_tensor(
                out=rot["S1"][:], in0=rot["S1"][:], in1=rot["Ac"][:], op=ALU.add),
                after=t_cs1)
            pi["S1"] = t_S1
            pi["Ac"] = t_S1
            t_mk = b.emit("vector", lambda e, _r=r: e.tensor_tensor(
                out=rot["mkb"][:], in0=rot["pqb"][:], in1=kbufs[_r][:], op=ALU.mult),
                after=t_cs1 + t_k + [pi.get("mkb")])
            t_mkT = transpose8(rot["mkb"], rot["mkTb"], [t_mk, pi.get("mkTb")])
            t_ka = proj(None, rot["mkTb"], "wka", io_rows["bka"], rows["bka"],
                        t_mkT + [backs["qw"]], banks=(2, 3), scale=1.0 / 32.0,
                        dst_f32=qasb)
            pi[("kb", r)] = t_mk
            pi["mkTb"] = t_ka[1]
            pi["pqb"] = t_mk
            backs["ka"] = t_ka
            backs["cs1"] = t_cs1
            backs["mk"] = t_mk
            if c == 0:
                tap("pqb", rot["pqb"][:], t_cs1, [128, D], BF16)
                tap("mkb", rot["mkb"][:], [t_mk], [128, D], BF16)

        def B3a(c):
            r = c % 2
            qasb = qas[:, r * D:(r + 1) * D]
            t_ka = backs["ka"]
            t_nm2 = b.emit("vector", lambda e, _q=qasb: e.tensor_reduce(
                out=nmr[:], in_=_q, axis=AX.X, op=ALU.max, negate=True),
                after=t_ka)
            t_kw = b.emit("scalar", lambda e, _q=qasb: e.activation(
                out=rot["qwb"][:], in_=_q, func=AF.Exp, bias=nmr[:, 0:1], scale=1.0,
                accum_out=ssr[:, 0:1]), after=[t_nm2, backs["A"]])
            t_sr2 = b.emit("vector", lambda e: e.reciprocal(srr[:], ssr[:]), after=[t_kw])
            t_B = b.emit("vector", lambda e: e.scalar_tensor_tensor(
                out=rot["Bc"][:], in0=rot["qwb"][:], scalar=srr[:, 0:1], in1=rot["mkb"][:],
                op0=ALU.mult, op1=ALU.mult), after=[pi.get("Bc")], guard=[t_sr2])
            pi[("qas", r)] = t_kw
            pi["mkb"] = t_B
            pi["qwb"] = t_B
            pi["nmr"] = t_kw
            backs["B"] = t_B
            if c == 0:
                tap("Bc", rot["Bc"][:], [t_B], [128, D], BF16)

        def B3b(c):
            r = c % 2
            xc = xres[:, c * D:(c + 1) * D]
            t_v = front[c][2]
            t_B = backs["B"]
            t_cs2 = []
            for s in range(2):
                emits = [
                    lambda e, bk, st, sp, _s=s: e.matmul(
                        bk[:], u128b[:], rot["Bc"][:, _s * 512:(_s + 1) * 512],
                        start=True, stop=False),
                    lambda e, bk, st, sp, _s=s: e.matmul(
                        bk[:], ones128b[:], rot["S2"][:, _s * 512:(_s + 1) * 512],
                        start=False, stop=True),
                ]
                _, ev = mm_group((4, 5)[s], emits,
                                 extra_after=[t_B, t_s2, pi.get("S2")],
                                 evac=lambda e, bk, _s=s, _c=c: e.activation(
                                     out=rot["pkb"][:, _s * 512:(_s + 1) * 512], in_=bk[:],
                                     func=AF.Copy, bias=0.0, scale=rt[:, _c:_c + 1]))
                t_cs2.append(ev)
            t_S2 = b.emit("vector", lambda e: e.tensor_tensor(
                out=rot["S2"][:], in0=rot["S2"][:], in1=rot["Bc"][:], op=ALU.add),
                after=t_cs2)
            pi["S2"] = t_S2
            t_tmp = b.emit("vector", lambda e, _r=r: e.tensor_tensor(
                out=tmps[:], in0=rot["pkb"][:], in1=vbufs[_r][:], op=ALU.mult),
                after=t_cs2 + t_v)
            t_x2 = b.emit("vector", lambda e, _x=xc: e.tensor_tensor(
                out=_x, in0=_x, in1=tmps[:], op=ALU.add), guard=[t_tmp])
            x2_ticks.append(t_x2)
            pi[("vb", r)] = t_tmp
            pi["pkb"] = t_tmp
            pi["Bc"] = t_S2
            if c == 0:
                tap("pkb", rot["pkb"][:], t_cs2, [128, D], BF16)
                tap("x2", xres[:, 0:D], [t_x2], [128, D])

        t_S1_last = [None]

        def B2_wrap(c):
            B2(c)
            t_S1_last[0] = pi["S1"]

        F1V(0)
        F1T(0)
        F2(0)
        for c in range(NT):
            B1(c)
            if c + 1 < NT:
                F1V(c + 1)
            if c >= 1:
                B3b(c - 1)
            if c + 1 < NT:
                F1T(c + 1)
            B2_wrap(c)
            B3a(c)
            if c + 1 < NT:
                F2(c + 1)
        B3b(NT - 1)

        attn_done = [("T", b.cnt["T"]), ("V", b.cnt["V"]), ("S", b.cnt["S"])]
        qas_g.__exit__(None, None, None)
        for g in reversed(list(rot_g.values())):
            g.__exit__(None, None, None)
        for g in reversed(list(awb_g.values())):
            g.__exit__(None, None, None)
        tmps_g.__exit__(None, None, None)
        for g in reversed(list(rows_g.values())):
            g.__exit__(None, None, None)

        ln2T = es.enter_context(nc.sbuf_tensor("ln2T", [128, ND * T], BF16))
        hid = es.enter_context(nc.sbuf_tensor("hid", [128, NDF * TH], BF16))
        wdb = es.enter_context(nc.sbuf_tensor("wdb", [128, NDF * D], BF16))
        wgq = [es.enter_context(nc.sbuf_tensor(f"wgq{i}", [128, ND * 128], BF16))
               for i in range(2)]
        wuq = [es.enter_context(nc.sbuf_tensor(f"wuq{i}", [128, ND * 128], BF16))
               for i in range(2)]
        ln2Te = es.enter_context(nc.sbuf_tensor("ln2Te", [128, ND * CAP], BF16))
        ye = xres
        idx16 = es.enter_context(nc.sbuf_tensor("idx16", [128, 8 * (CAP // 16)], I16))
        wgbuf = [es.enter_context(nc.sbuf_tensor(f"wgbuf{i}", [128, (CAP // 128) * 64],
                                                 F32)) for i in range(2)]
        gateb = es.enter_context(nc.sbuf_tensor("gateb", [128, ND * 8], BF16))
        ln2c = es.enter_context(nc.sbuf_tensor("ln2c", [128, D], BF16))
        sgb = es.enter_context(nc.sbuf_tensor("sgb", [128, TH], BF16))
        sqscrB = sgb
        LG = es.enter_context(nc.sbuf_tensor("LG", [128, NT * 8], F32))
        WG8 = es.enter_context(nc.sbuf_tensor("WG8", [128, NT * 64], F32))
        SEL = es.enter_context(nc.sbuf_tensor("SEL", [128, NT * 8], F32))
        identf = es.enter_context(nc.sbuf_tensor("identf", [128, 128], F32))
        ones16 = es.enter_context(nc.sbuf_tensor("ones16", [1, 16], F32))
        slotw = es.enter_context(nc.sbuf_tensor("slotw", [16, CAP // 16], F32))
        nfall = es.enter_context(nc.sbuf_tensor("nfall", [1, 8], U32))
        nff = es.enter_context(nc.sbuf_tensor("nff", [1, 1], F32))
        nfb = es.enter_context(nc.sbuf_tensor("nfb", [16, 1], F32))
        valtm = es.enter_context(nc.sbuf_tensor("valtm", [128, NT], F32))
        valw = es.enter_context(nc.sbuf_tensor("valw", [16, 128], F32))
        idxfw = es.enter_context(nc.sbuf_tensor("idxfw", [16, 128], F32))
        mk16 = es.enter_context(nc.sbuf_tensor("mk16", [16, CAP // 16], F32))
        idxm = es.enter_context(nc.sbuf_tensor("idxm", [16, CAP // 16], F32))
        t_identf = b.emit("vector", lambda e: e.tensor_scalar(
            out=identf[:], in0=Ft[:], scalar1=Pc[:, 0:1], scalar2=None,
            op0=ALU.is_equal), after=[t_iF, t_iP] + attn_done)
        t_ones16 = b.emit("gpsimd", lambda e: e.memset(ones16[:], 1.0),
                          after=attn_done)
        t_slotw = b.emit("gpsimd", lambda e: e.iota(
            slotw[:], [[16, CAP // 16]], base=0, channel_multiplier=1,
            allow_small_or_imprecise_dtypes=True), after=attn_done)
        t_wg8z = b.emit("gpsimd", lambda e: e.memset(WG8[:], 0.0),
                        after=attn_done)

        t_gw = b.emit("gpsimd", lambda e: e.dma_start(
            out=gateb[:], in_=dw["gatew"].rearrange("(c p) f -> p c f", p=128)),
            dma=True, after=attn_done)

        pc = {}
        gate_ticks = []
        lnw_ticks = []
        ln2_done = None
        t_sum_last = t_sq_last = None
        for c in range(NT):
            xc0 = xres[:, c * D:(c + 1) * D]
            t_sum_last = b.emit("vector", lambda e, _c=c, _x=xc0: e.tensor_reduce(
                out=sumc[:, _c:_c + 1], in_=_x, axis=AX.X, op=ALU.add),
                after=[x2_ticks[c]])
            t_sq_last = b.emit("scalar", lambda e, _c=c, _x=xc0: e.activation(
                out=sqscrB[:], in_=_x, func=AF.Square, bias=0.0, scale=1.0,
                accum_out=sqc[:, _c:_c + 1]), after=[x2_ticks[c]] + attn_done,
                guard=[t_sq_last])
        t_mu = b.emit("vector", lambda e: e.tensor_scalar(
            out=mu2c[:], in0=sumc[:], scalar1=1.0 / D, scalar2=None, op0=ALU.mult),
            guard=[t_sum_last])
        t_msq = b.emit("vector", lambda e: e.tensor_tensor(
            out=vp16[:], in0=mu2c[:], in1=mu2c[:], op=ALU.mult), guard=[t_mu])
        t_va = b.emit("vector", lambda e: e.scalar_tensor_tensor(
            out=vp16[:], in0=sqc[:], scalar=1.0 / D, in1=vp16[:],
            op0=ALU.mult, op1=ALU.subtract), after=[t_sq_last], guard=[t_msq])
        t_vp = b.emit("vector", lambda e: e.tensor_scalar(
            out=vp16[:], in0=vp16[:], scalar1=EPS, scalar2=None, op0=ALU.add),
            guard=[t_va])
        t_ln = b.emit("scalar", lambda e: e.activation(
            out=rs2c[:], in_=vp16[:], func=AF.Ln, bias=0.0, scale=1.0), after=[t_vp])
        t_rs = b.emit("scalar", lambda e: e.activation(
            out=rs2c[:], in_=rs2c[:], func=AF.Exp, bias=0.0, scale=-0.5), guard=[t_ln])
        for c in range(NT):
            xc = xres[:, c * D:(c + 1) * D]
            t_n = b.emit("vector", lambda e, _c=c, _x=xc: e.tensor_scalar(
                out=ln2c[:], in0=_x, scalar1=mu2c[:, _c:_c + 1],
                scalar2=rs2c[:, _c:_c + 1], op0=ALU.subtract, op1=ALU.mult),
                after=[t_rs, pc.get("ln2c"), pc.get("ln2c_evt")] + attn_done,
                guard=[t_mu])
            evt = []
            for half in range(2):
                emits = []
                for j in range(4):
                    emits.append(lambda e, bk, st, sp, _j=j, _dc=half * 4 + j: e.matmul(
                        bk[:, _j * 128:(_j + 1) * 128],
                        ln2c[:, _dc * 128:(_dc + 1) * 128], identb[:],
                        start=st, stop=sp))
                bank_i = (6, 7)[half]
                bank = pb[bank_i]
                gt = None
                n = len(emits)
                for i2, mk in enumerate(emits):
                    aft = [t_n, t_ident, bank_last[bank_i]] if i2 == 0 else ()
                    gt = b.emit("tensor",
                                lambda e, _mk=mk, _bk=bank, _st=(i2 == 0),
                                _sp=(i2 == n - 1): _mk(e, _bk, _st, _sp),
                                after=aft, inc=(i2 == n - 1))
                for j in range(4):
                    dc = half * 4 + j
                    evt.append(b.emit("scalar", lambda e, _dc=dc, _j=j, _bk=bank, _c=c: e.copy(
                        ln2T[:, _dc * T + _c * 128:_dc * T + (_c + 1) * 128],
                        _bk[:, _j * 128:(_j + 1) * 128]), after=[gt]))
                bank_last[bank_i] = evt[-1]
            t_lnw = b.emit("sync", lambda e, _c=c: e.dma_start(
                out=lnhbm.rearrange("(c p) f -> p c f", p=128)[:, _c:_c + 1, :],
                in_=ln2c[:]), dma=True, after=[t_n])
            lnw_ticks.append(t_lnw)
            pc["ln2c"] = t_lnw
            pc["ln2c_evt"] = evt[-1]
            emits = []
            for dc in range(ND):
                emits.append(lambda e, bk, st, sp, _dc=dc, _c=c: e.matmul(
                    bk[:, 0:8], ln2T[:, _dc * T + _c * 128:_dc * T + (_c + 1) * 128],
                    gateb[:, _dc * 8:(_dc + 1) * 8], start=st, stop=sp))
            _, t_lg = mm_group(0, emits, extra_after=[t_gw] + evt,
                               evac=lambda e, bk, _c=c: e.tensor_copy(
                                   LG[:, _c * 8:(_c + 1) * 8], bk[:, 0:8]),
                               evac_eng="vector")
            lgc = LG[:, c * 8:(c + 1) * 8]
            t_m1 = b.emit("vector", lambda e, _l=lgc: e.tensor_reduce(
                out=nm1c[:], in_=_l, axis=AX.X, op=ALU.max, negate=True),
                after=[pc.get("g_nm1")], guard=[t_lg])
            t_msk = b.emit("vector", lambda e, _l=lgc: e.tensor_scalar(
                out=selc[:], in0=_l, scalar1=nm1c[:, 0:1], scalar2=0.0,
                op0=ALU.add, op1=ALU.is_ge), guard=[t_m1], after=[pc.get("g_sel")])
            t_lp = b.emit("vector", lambda e, _l=lgc: e.scalar_tensor_tensor(
                out=selc[:], in0=selc[:], scalar=-1e30, in1=_l,
                op0=ALU.mult, op1=ALU.add), guard=[t_msk])
            t_m2 = b.emit("vector", lambda e: e.tensor_reduce(
                out=nm2c[:], in_=selc[:], axis=AX.X, op=ALU.max, negate=True),
                after=[pc.get("g_nm2")], guard=[t_lp])
            t_sel = b.emit("vector", lambda e, _l=lgc, _c=c: e.tensor_scalar(
                out=SEL[:, _c * 8:(_c + 1) * 8], in0=_l, scalar1=nm2c[:, 0:1],
                scalar2=0.0, op0=ALU.add, op1=ALU.is_ge), guard=[t_m2])
            t_E = b.emit("scalar", lambda e, _l=lgc, _c=c: e.activation(
                out=WG8[:, _c * 64:_c * 64 + 8], in_=_l, func=AF.Exp, bias=nm1c[:, 0:1],
                scale=1.0), after=[t_m1, t_lp, pc.get("g_E"), t_wg8z])
            t_ed = b.emit("scalar", lambda e: e.activation(
                out=edc[:], in_=nm2c[:], func=AF.Exp, bias=nm1c[:, 0:1], scale=-1.0),
                after=[t_m2, pc.get("g_ed")])
            t_den = b.emit("vector", lambda e: e.tensor_scalar(
                out=edc[:], in0=edc[:], scalar1=1.0, scalar2=None, op0=ALU.add),
                after=[t_ed])
            t_rd = b.emit("vector", lambda e: e.reciprocal(nm1c[:], edc[:]),
                          guard=[t_den])
            t_w = b.emit("vector", lambda e, _c=c: e.scalar_tensor_tensor(
                out=WG8[:, _c * 64:_c * 64 + 8], in0=WG8[:, _c * 64:_c * 64 + 8],
                scalar=nm1c[:, 0:1], in1=SEL[:, _c * 8:(_c + 1) * 8],
                op0=ALU.mult, op1=ALU.mult),
                after=[t_E], guard=[t_rd, t_sel])
            gate_ticks.append(t_w)
            if c == 0:
                tap("ln2c", ln2T[:, 0:128], [evt[0]], [128, 128], BF16)
                tap("LG", LG[:, 0:8], [t_lg], [128, 8])
                tap("WG", WG8[:, 0:8], [t_w], [128, 8])
            pc["g_nm1"] = t_w
            pc["g_sel"] = t_w
            pc["g_nm2"] = t_E
            pc["g_E"] = t_w
            pc["g_ed"] = t_w
            ln2_done = evt[-1]

        t_whw = b.emit("sync", lambda e: e.dma_start(
            out=whbm.rearrange("(c p) f -> p c f", p=128)[:, :, :],
            in_=WG8.rearrange("p (c f) -> p c f", f=64)[:, :, :]),
            dma=True, after=[gate_ticks[-1]])

        regcell = [None, None]

        def get_reg(e, i):
            if regcell[i] is None:
                regcell[i] = e.alloc_register(name=f"nfreg{i}")
            return regcell[i]

        route_ticks = []
        prev_tT = None
        for ei in range(8):
            tV1 = b.emit("vector", lambda e, _e=ei: e.tensor_tensor(
                out=valtm[:], in0=SEL.rearrange("p (c f) -> p c f", f=8)[:, :, _e:_e + 1],
                in1=tvals[:], op=ALU.mult), guard=[gate_ticks[-1]],
                after=[t_itv, prev_tT])
            tV2 = b.emit("vector", lambda e: e.tensor_scalar(
                out=valtm[:], in0=valtm[:], scalar1=1.0, scalar2=None,
                op0=ALU.subtract), guard=[tV1])
            tT = b.emit("tensor", lambda e: e.matmul(
                pb[6][0:16, 0:128], valtm[:], identf[:], start=True, stop=True),
                after=[tV2, t_identf, bank_last[6]])
            prev_tT = tT
            tS = b.emit("scalar", lambda e: e.copy(valw[:], pb[6][0:16, 0:128]),
                        after=[tT])
            bank_last[6] = tS
            tG = b.emit("gpsimd", lambda e, _e=ei: e.sparse_gather(
                out=idxfw[:], in_=valw[:], num_found=nfall[0:1, _e:_e + 1]),
                after=[tS])
            tV3 = b.emit("vector", lambda e, _e=ei: e.tensor_copy(
                nff[:], nfall[0:1, _e:_e + 1]), after=[tG])
            tT2 = b.emit("tensor", lambda e: e.matmul(
                pb[6][0:16, 0:1], ones16[0:1, :], nff[0:1, 0:1], start=True,
                stop=True),
                after=[tV3, t_ones16, bank_last[6]])
            tS2 = b.emit("scalar", lambda e: e.copy(nfb[:], pb[6][0:16, 0:1]),
                         after=[tT2])
            bank_last[6] = tS2
            tV4 = b.emit("vector", lambda e: e.tensor_scalar(
                out=mk16[:], in0=slotw[:], scalar1=nfb[0:16, 0:1], scalar2=None,
                op0=ALU.is_lt), after=[tS2, t_slotw])
            tV5 = b.emit("vector", lambda e: e.tensor_scalar(
                out=idxm[:], in0=idxfw[0:16, 0:CAP // 16], scalar1=1.0,
                scalar2=None, op0=ALU.add), after=[tG])
            tV6 = b.emit("vector", lambda e: e.tensor_tensor(
                out=idxm[:], in0=idxm[:], in1=mk16[:], op=ALU.mult),
                guard=[tV5, tV4])
            tV7 = b.emit("vector", lambda e: e.tensor_scalar(
                out=idxm[:], in0=idxm[:], scalar1=1.0, scalar2=None,
                op0=ALU.subtract), guard=[tV6])
            tV8 = b.emit("vector", lambda e, _e=ei: e.tensor_copy(
                idx16[0:16, _e * (CAP // 16):(_e + 1) * (CAP // 16)], idxm[:]),
                guard=[tV7])
            route_ticks.append(tV8)
        tap("nfall", nfall[0:1, :], [route_ticks[-1]], [1, 8], U32)
        tap("idx16", idx16[0:16, :], [route_ticks[-1]], [16, 8 * (CAP // 16)], I16)
        tap("SELt", SEL[:, 0:128], [gate_ticks[-1]], [128, 128])
        t_repl = None
        for rr in range(1, 8):
            t_repl = b.emit("gpsimd", lambda e, _r=rr: e.dma_start(
                out=idx16[_r * 16:(_r + 1) * 16, :], in_=idx16[0:16, :]),
                dma=True, after=[route_ticks[-1]])

        t_wd_lastuse = None
        ring_lastuse = {}
        ring_tick = {}
        hid_ticks = {}
        t_hid_lastuse = None
        last_ev = None

        t_store = None
        t_scatter = None
        wg_lastuse = [None, None]
        t_gathe = {}
        NS = CAP // 128

        def emit_gathers(e_next, a_done_tick):
            buf = e_next % 2
            ri = e_next % 2
            t_rl = b.emit("gpsimd", lambda e, _e=e_next, _i=ri: e.reg_load(
                get_reg(e, _i), nfall[0:1, _e - 1:_e]),
                after=[route_ticks[e_next - 1], t_repl], inc=False)
            t_mz = b.emit("gpsimd", lambda e: e.memzero(ln2Te[:]),
                          after=[a_done_tick])
            t_ga = b.emit("gpsimd", lambda e, _e=e_next, _i=ri: e.dma_gather(
                out_ap=ln2Te.rearrange("p (c f) -> p c f", f=CAP)[:, :, :],
                in_ap=lnhbm[:, :],
                idxs_ap=idx16[:, (_e - 1) * (CAP // 16):_e * (CAP // 16)],
                num_idxs=CAP, num_idxs_reg=regcell[_i], elem_size=D,
                transpose=True),
                dma=True, after=[lnw_ticks[-1], a_done_tick, t_mz])
            t_mz2 = b.emit("gpsimd", lambda e, _b=buf: e.memzero(wgbuf[_b][:]),
                           after=[wg_lastuse[buf]])
            t_gw2 = b.emit("gpsimd", lambda e, _e=e_next, _b=buf, _i=ri: e.dma_gather(
                out_ap=wgbuf[_b].rearrange("p (c f) -> p c f", f=64)[:, :, :],
                in_ap=whbm[:, :],
                idxs_ap=idx16[:, (_e - 1) * (CAP // 16):_e * (CAP // 16)],
                num_idxs=CAP, num_idxs_reg=regcell[_i], elem_size=64,
                transpose=False),
                dma=True, after=[t_whw, wg_lastuse[buf], t_mz2])
            t_gathe[e_next] = (t_ga, t_gw2, buf)

        def _e_r(e):
            return e - 1

        for e_i in range(1 + N_ROUTED):
            routed = e_i >= 1
            nth = 1 if routed else NTH
            t_wd_load = None
            if routed:
                t_ga, t_gw2, wbuf = t_gathe[e_i]
            for th in range(nth):
                for dck in range(NDF):
                    buf = dck % 2
                    ring_tick[("g", buf)] = b.emit("gpsimd",
                        lambda e, _e=e_i, _d=dck, _b=buf: e.dma_start(
                            out=wgq[_b][:],
                            in_=dw[f"wg{_e}"][:, _d * 128:(_d + 1) * 128]
                                .rearrange("(c p) f -> p c f", p=128)),
                        dma=True, after=[ring_lastuse.get(("g", buf))] + attn_done)
                    ring_tick[("u", buf)] = b.emit("gpsimd",
                        lambda e, _e=e_i, _d=dck, _b=buf: e.dma_start(
                            out=wuq[_b][:],
                            in_=dw[f"wu{_e}"][:, _d * 128:(_d + 1) * 128]
                                .rearrange("(c p) f -> p c f", p=128)),
                        dma=True, after=[ring_lastuse.get(("u", buf))] + attn_done)
                    if t_wd_load is None:
                        t_wd_load = b.emit("gpsimd", lambda e, _e=e_i: e.dma_start(
                            out=wdb[:], in_=dw[f"wd{_e}"].rearrange(
                                "(c p) f -> p c f", p=128)),
                            dma=True, after=[t_wd_lastuse] + attn_done)
                    par = dck % 2
                    gbi = (0, 1) if par == 0 else (4, 5)
                    ubi = (2, 3) if par == 0 else (6, 7)
                    g_t, u_t = [], []
                    for s in range(2):
                        for kind, bis, wq_ in (("g", gbi, wgq[buf]),
                                               ("u", ubi, wuq[buf])):
                            emits = []
                            for dc in range(ND):
                                if routed:
                                    emits.append(lambda e, bk, st, sp, _dc=dc,
                                                 _s=s, _w=wq_: e.matmul(
                                        bk[:],
                                        _w[:, _dc * 128:(_dc + 1) * 128],
                                        ln2Te[:, _dc * CAP + _s * 512:
                                              _dc * CAP + (_s + 1) * 512],
                                        start=st, stop=sp))
                                else:
                                    emits.append(lambda e, bk, st, sp, _dc=dc,
                                                 _s=s, _th=th, _w=wq_: e.matmul(
                                        bk[:],
                                        _w[:, _dc * 128:(_dc + 1) * 128],
                                        ln2T[:, _dc * T + _th * TH + _s * 512:
                                             _dc * T + _th * TH + (_s + 1) * 512],
                                        start=st, stop=sp))
                            aft = [ring_tick[(kind, buf)], ln2_done,
                                   gate_ticks[-1]]
                            if routed:
                                aft.append(t_ga)
                            gt, _ = mm_group(bis[s], emits, extra_after=aft)
                            (g_t if kind == "g" else u_t).append(gt)
                    for s in range(2):
                        t_sg = b.emit("scalar", lambda e, _s=s, _bk=pb[gbi[s]]: e.activation(
                            out=sgb[:, _s * 512:(_s + 1) * 512], in_=_bk[:],
                            func=AF.Silu, bias=0.0, scale=1.0),
                            after=[g_t[s], pc.get("sgb%d" % s)])
                        bank_last[gbi[s]] = t_sg
                        t_h = b.emit("vector", lambda e, _s=s, _dck=dck,
                                     _bk=pb[ubi[s]]: e.tensor_tensor(
                            out=hid[:, _dck * TH + _s * 512:_dck * TH + (_s + 1) * 512],
                            in0=sgb[:, _s * 512:(_s + 1) * 512], in1=_bk[:],
                            op=ALU.mult),
                            after=[u_t[s], t_sg, t_hid_lastuse])
                        bank_last[ubi[s]] = t_h
                        pc["sgb%d" % s] = t_h
                        hid_ticks[(dck, s)] = t_h
                    ring_lastuse[("g", buf)] = g_t[-1]
                    ring_lastuse[("u", buf)] = u_t[-1]
                if th == nth - 1 and e_i < N_ROUTED:
                    emit_gathers(e_i + 1, u_t[-1])
                down_last = []
                ye_evs = []
                for tc in range(8):
                    gc = th * 8 + tc
                    bi = (0, 1) if tc % 2 == 0 else (2, 3)
                    for s in range(2):
                        emits = []
                        for dck in range(NDF):
                            emits.append(lambda e, bk, st, sp, _dck=dck, _tc=tc,
                                         _s=s: e.matmul(
                                bk[:],
                                hid[:, _dck * TH + _tc * 128:_dck * TH + (_tc + 1) * 128],
                                wdb[:, _dck * D + _s * 512:_dck * D + (_s + 1) * 512],
                                start=st, stop=sp))
                        hid_all = [hid_ticks[(d2, s2)] for d2 in range(NDF)
                                   for s2 in range(2)] if (tc == 0 and s == 0) else []
                        if not routed:
                            xslice = xres[:, gc * D + s * 512:gc * D + (s + 1) * 512]
                            evac = lambda e, bk, _x=xslice: e.tensor_tensor(
                                out=_x, in0=bk[:], in1=_x, op=ALU.add)
                            eg = []
                        else:
                            ysl = ye[:, tc * D + s * 512:tc * D + (s + 1) * 512]
                            evac = lambda e, bk, _y=ysl, _tc=tc, _e=e_i, _b=wbuf:                                 e.tensor_scalar(
                                    out=_y, in0=bk[:],
                                    scalar1=wgbuf[_b][:, _tc * 64 + (_e - 1):
                                                      _tc * 64 + _e],
                                    scalar2=None, op0=ALU.mult)
                            eg = [t_gw2, t_scatter, t_store]
                        gt, ev = mm_group(bi[s], emits,
                                          extra_after=[t_wd_load] + hid_all,
                                          evac=evac, evac_eng="vector",
                                          evac_guard=eg)
                        down_last.append(gt)
                        ye_evs.append(ev)
                        last_ev = ev
                t_hid_lastuse = down_last[-1]
                if routed:
                    wg_lastuse[wbuf] = ye_evs[-1]
            t_wd_lastuse = down_last[-1]
            if not routed:
                t_store = b.emit("sync", lambda e: e.dma_start(
                    out=dy.rearrange("(c p) f -> p c f", p=128), in_=xres[:]),
                    dma=True, after=[last_ev])
            else:
                t_scatter = b.emit("gpsimd", lambda e, _e=e_i: e.dma_scatter_add(
                    out_ap=dy[:, :],
                    in_ap=ye[:, 0:NS * D].rearrange("p (c f) -> p c f", f=D)[:, :, :],
                    idxs_ap=idx16[:, (_e - 1) * (CAP // 16):_e * (CAP // 16)],
                    num_idxs=CAP, num_idxs_reg=regcell[_e % 2], elem_size=D),
                    dma=True,
                    after=[t_store, t_scatter, ye_evs[-1], t_repl])

        def final_wait(e, _f=t_scatter, _g=t_store):
            e.wait_ge(b.sems[_g[0]], _g[1])
            if _f is not None:
                e.wait_ge(b.sems[_f[0]], _f[1])
        b.ev["sync"].append(final_wait)

        block = es.enter_context(nc.Block())

        @block.tensor
        def _(t):
            for fn in b.ev["tensor"]:
                fn(t)

        @block.vector
        def _(v):
            for fn in b.ev["vector"]:
                fn(v)

        @block.scalar
        def _(s):
            for fn in b.ev["scalar"]:
                fn(s)

        @block.gpsimd
        def _(g):
            for fn in b.ev["gpsimd"]:
                fn(g)

        @block.sync
        def _(sy):
            for fn in b.ev["sync"]:
                fn(sy)

    import bass_rust as _bass_rust
    from concourse.library_config import all_libraries, standard
    _mask = {}
    for _lib in all_libraries:
        for _it in _lib.instructions:
            _mask[_it] = _mask.get(_it, 0) | (1 << _lib.index)
    _bass_rust.insert_library_loads(nc, _mask, len(all_libraries), standard.index)
    mybir.codegen_inst_isa_subclasses(nc)
    return nc


def _in_map(inputs, bi):
    g = np.ascontiguousarray
    m = {
        "x": inputs["L0_attn_hidden_states"][bi],
        "wq": inputs["L0_attn_Wq"], "wk": inputs["L0_attn_Wk"],
        "wv": inputs["L0_attn_Wv"], "wqa": inputs["L0_attn_Wqa"],
        "wka": inputs["L0_attn_Wka"],
        "bq": inputs["L0_attn_bq"].reshape(1, D),
        "bk": inputs["L0_attn_bk"].reshape(1, D),
        "bv": inputs["L0_attn_bv"].reshape(1, D),
        "bqa": inputs["L0_attn_bqa"].reshape(1, D),
        "bka": inputs["L0_attn_bka"].reshape(1, D),
        "gcol": inputs["L0_attn_gamma"].reshape(ND, 128).T,
        "bcol": inputs["L0_attn_beta"].reshape(ND, 128).T,
        "gatew": inputs["L0_ffn_gate_w"],
        "wg0": inputs["L0_ffn_ws_gate"], "wu0": inputs["L0_ffn_ws_up"],
        "wd0": inputs["L0_ffn_ws_down"],
    }
    for e in range(8):
        m[f"wg{e + 1}"] = inputs["L0_ffn_w_gate"][e]
        m[f"wu{e + 1}"] = inputs["L0_ffn_w_up"][e]
        m[f"wd{e + 1}"] = inputs["L0_ffn_w_down"][e]
    return {k: g(np.asarray(v, dtype=np.float32)) for k, v in m.items()}


def kernel(**inputs):
    nc = build_program()
    in_maps = [_in_map(inputs, bi) for bi in range(8)]
    res = run_bass_kernel_spmd(nc, in_maps, core_ids=list(range(8)))
    out = np.stack([res.results[i]["y"] for i in range(8)], axis=0)
    return np.ascontiguousarray(out.astype(np.float32))


# version 15
# speedup vs baseline: 1.0169x; 1.0169x over previous
import numpy as np
from contextlib import ExitStack
import concourse.bass as bass
import concourse.mybir as mybir
from concourse.bass_utils import run_bass_kernel_spmd

F32 = mybir.dt.float32
BF16 = mybir.dt.bfloat16
I16 = mybir.dt.int16
U32 = mybir.dt.uint32
AF = mybir.ActivationFunctionType
ALU = mybir.AluOpType
AX = mybir.AxisListType

T, NT = 2048, 16
CAP = 1024
D, ND = 1024, 8
DFF, NDF = 2048, 16
TH, NTH = 1024, 2
EPS = 1e-5

ENGINES = ("tensor", "vector", "scalar", "gpsimd", "sync")
ESEM = {"tensor": "T", "vector": "V", "scalar": "S", "gpsimd": "G", "sync": "Y"}


class Bld:

    NWD = 12
    NIO = 4

    def __init__(self):
        self.ev = {e: [] for e in ENGINES}
        self.cnt = {}
        self.lastw = {e: {} for e in ENGINES}
        self.sems = {}
        self.wd_rr = 0
        self.io_rr = 0

    def emit(self, eng, mk, after=(), guard=(), dma=False, inc=True):
        lane_prev = None
        if dma:
            if eng == "gpsimd":
                sem = f"WD{self.wd_rr % self.NWD}"
                self.wd_rr += 1
            else:
                sem = f"IO{self.io_rr % self.NIO}"
                self.io_rr += 1
            if self.cnt.get(sem, 0) > 0:
                lane_prev = (sem, self.cnt[sem])
        else:
            sem = ESEM[eng]
        want = {}
        if lane_prev is not None:
            want[lane_prev[0]] = lane_prev[1]
        for t in after:
            if t is None:
                continue
            s, v = t
            if want.get(s, 0) < v:
                want[s] = v
        for t in guard:
            if t is None:
                continue
            s, v = t
            if want.get(s, 0) < v:
                want[s] = v
        lw = self.lastw[eng]
        ws = []
        for s, v in want.items():
            if lw.get(s, 0) >= v:
                continue
            lw[s] = v
            ws.append((s, v))
        tick = None
        n = 16 if dma else 1
        if inc:
            self.cnt[sem] = self.cnt.get(sem, 0) + n
            tick = (sem, self.cnt[sem])
        B = self

        def fn(e, _ws=tuple(ws), _mk=mk, _sem=sem, _n=n, _inc=inc):
            for s, v in _ws:
                e.wait_ge(B.sems[s], v)
            ins = _mk(e)
            if _inc:
                ins.then_inc(B.sems[_sem], _n)
        self.ev[eng].append(fn)
        return tick


DEBUG = False
N_ROUTED = 8


def build_program():
    nc = bass.Bass("TRN2", target_bir_lowering=False)
    b = Bld()

    def tap(name, ap, after, shape, dtype=F32):
        if not DEBUG:
            return
        d = nc.dram_tensor("dbg_" + name, shape, dtype, kind="ExternalOutput").ap()
        b.emit("sync", lambda e: e.dma_start(out=d[:, :], in_=ap), dma=True,
               after=list(after))
    bank_last = [None] * 8

    dx = nc.dram_tensor("x", [T, D], F32, kind="ExternalInput").ap()
    dy = nc.dram_tensor("y", [T, D], F32, kind="ExternalOutput").ap()
    dw = {}
    for nm in ("wq", "wk", "wv", "wqa", "wka"):
        dw[nm] = nc.dram_tensor(nm, [D, D], F32, kind="ExternalInput").ap()
    for nm in ("bq", "bk", "bv", "bqa", "bka"):
        dw[nm] = nc.dram_tensor(nm, [1, D], F32, kind="ExternalInput").ap()
    dw["gcol"] = nc.dram_tensor("gcol", [128, ND], F32, kind="ExternalInput").ap()
    dw["bcol"] = nc.dram_tensor("bcol", [128, ND], F32, kind="ExternalInput").ap()
    dw["gatew"] = nc.dram_tensor("gatew", [D, 8], F32, kind="ExternalInput").ap()
    lnhbm = nc.dram_tensor("lnhbm", [T, D], BF16, kind="Internal").ap()
    whbm = nc.dram_tensor("whbm", [T, 64], F32, kind="Internal").ap()
    for e in range(9):
        dw[f"wg{e}"] = nc.dram_tensor(f"wg{e}", [D, DFF], F32, kind="ExternalInput").ap()
        dw[f"wu{e}"] = nc.dram_tensor(f"wu{e}", [D, DFF], F32, kind="ExternalInput").ap()
        dw[f"wd{e}"] = nc.dram_tensor(f"wd{e}", [DFF, D], F32, kind="ExternalInput").ap()

    es = ExitStack()

    def sb(name, shape, dt):
        return es.enter_context(nc.sbuf_tensor(name, shape, dt))

    with es:
        xres = sb("xres", [128, NT * D], F32)
        Ft = sb("Ft", [128, 128], F32)
        Pc = sb("Pc", [128, 1], F32)
        identb = sb("identb", [128, 128], BF16)
        u128b = sb("u128b", [128, 128], BF16)
        ones128b = sb("ones128b", [128, 128], BF16)
        onescol = sb("onescol", [1, 128], BF16)
        tvals = sb("tvals", [128, NT], F32)
        rt = sb("rt", [128, NT], F32)
        gcolv = sb("gcolv", [128, ND], F32)
        bcolv = sb("bcolv", [128, ND], F32)
        bcolb = sb("bcolb", [128, ND], BF16)
        sumc = sb("sumc", [128, NT], F32)
        sqc = sb("sqc", [128, NT], F32)
        muc = sb("muc", [128, NT], F32)
        rsc = sb("rsc", [128, NT], F32)
        mu2c = sb("mu2c", [128, NT], F32)
        rs2c = sb("rs2c", [128, NT], F32)
        vp16 = sb("vp16", [128, NT], F32)
        nm1c = sb("nm1c", [128, 1], F32)
        nm2c = sb("nm2c", [128, 1], F32)
        edc = sb("edc", [128, 1], F32)
        selc = sb("selc", [128, 8], F32)
        nmr = sb("nmr", [128, 1], F32)
        ssr = sb("ssr", [128, 1], F32)
        srr = sb("srr", [128, 1], F32)

        rows_g = {nm: nc.sbuf_tensor("r_" + nm, [1, D], BF16)
                  for nm in ("bq", "bk", "bv", "bqa", "bka")}
        rows = {nm: g.__enter__() for nm, g in rows_g.items()}
        tmps_g = nc.sbuf_tensor("tmps", [128, D], BF16)
        tmps = tmps_g.__enter__()
        awb_g = {nm: nc.sbuf_tensor("b_" + nm, [128, ND * D], BF16) for nm in
                 ("wq", "wk", "wv", "wqa", "wka")}
        awb = {nm: g.__enter__() for nm, g in awb_g.items()}
        rot_g = {}
        rot = {}
        for nm in ("nb", "nTb", "qb", "qb1", "qTb", "kb", "kb1", "vb", "vb1",
                   "mkb", "mkTb", "pqb", "pkb", "Ac", "Bc", "qwb", "S1", "S2"):
            rot_g[nm] = nc.sbuf_tensor("t_" + nm, [128, D], BF16)
            rot[nm] = rot_g[nm].__enter__()
        qas_g = nc.sbuf_tensor("qas", [128, 2 * D], F32)
        qas = qas_g.__enter__()
        qbufs = (rot["qb"], rot["qb1"])
        kbufs = (rot["kb"], rot["kb1"])
        vbufs = (rot["vb"], rot["vb1"])

        pb = [es.enter_context(nc.psum_tensor(f"pb{i}", [128, 512], F32)) for i in range(8)]

        for s in (["T", "V", "S", "G", "Y"]
                  + [f"WD{i}" for i in range(Bld.NWD)]
                  + [f"IO{i}" for i in range(Bld.NIO)]):
            b.sems[s] = es.enter_context(nc.semaphore("sem" + s))

        t_x = b.emit("sync", lambda e: e.dma_start(
            out=xres[:], in_=dx.rearrange("(c p) f -> p c f", p=128)), dma=True)
        io_rows = {}
        for nm in ("bq", "bk", "bv", "bqa", "bka"):
            io_rows[nm] = b.emit("gpsimd", lambda e, a=rows[nm], d=dw[nm]: e.dma_start(
                out=a[:], in_=d[:, :]), dma=True)
        t_gcol = b.emit("sync", lambda e: e.dma_start(out=gcolv[:], in_=dw["gcol"][:, :]), dma=True)
        t_bcol = b.emit("sync", lambda e: e.dma_start(out=bcolv[:], in_=dw["bcol"][:, :]), dma=True)

        t_iF = b.emit("gpsimd", lambda e: e.iota(
            Ft[:], [[1, 128]], base=0, channel_multiplier=0,
            allow_small_or_imprecise_dtypes=True))
        t_iP = b.emit("gpsimd", lambda e: e.iota(
            Pc[:], [[0, 1]], base=0, channel_multiplier=1,
            allow_small_or_imprecise_dtypes=True))
        t_itv = b.emit("gpsimd", lambda e: e.iota(
            tvals[:], [[128, NT]], base=1, channel_multiplier=1,
            allow_small_or_imprecise_dtypes=True))
        t_ones = b.emit("gpsimd", lambda e: e.memset(ones128b[:], 1.0))
        t_onescol = b.emit("gpsimd", lambda e: e.memset(onescol[:], 1.0))
        t_s1 = b.emit("gpsimd", lambda e: e.memset(rot["S1"][:], 0.0))
        t_s2 = b.emit("gpsimd", lambda e: e.memset(rot["S2"][:], 0.0))

        t_ident = b.emit("vector", lambda e: e.tensor_scalar(
            out=identb[:], in0=Ft[:], scalar1=Pc[:, 0:1], scalar2=None,
            op0=ALU.is_equal), after=[t_iF, t_iP])
        t_u128 = b.emit("vector", lambda e: e.tensor_scalar(
            out=u128b[:], in0=Ft[:], scalar1=Pc[:, 0:1], scalar2=None,
            op0=ALU.is_ge), after=[t_iF, t_iP])
        t_rt = b.emit("vector", lambda e: e.reciprocal(rt[:], tvals[:]), after=[t_itv])
        t_bcolb = b.emit("vector", lambda e: e.tensor_copy(bcolb[:], bcolv[:]), after=[t_bcol])

        t_aw = {}
        for nm in ("wq", "wk", "wv", "wqa", "wka"):
            t_aw[nm] = b.emit("gpsimd", lambda e, a=awb[nm], d=dw[nm]: e.dma_start(
                out=a[:], in_=d.rearrange("(c p) f -> p c f", p=128)), dma=True)

        def mm_group(bank_i, emits, extra_after=(), evac=None, evac_eng="scalar",
                     evac_guard=()):
            bank = pb[bank_i]
            n = len(emits)
            gt = None
            for i, mk in enumerate(emits):
                aft = (list(extra_after) + [bank_last[bank_i]]) if i == 0 else ()
                gt = b.emit("tensor",
                            lambda e, _mk=mk, _bk=bank, _st=(i == 0), _sp=(i == n - 1): _mk(
                                e, _bk, _st, _sp),
                            after=aft, inc=(i == n - 1))
            ev = None
            if evac is not None:
                ev = b.emit(evac_eng, lambda e, _mk=evac, _bk=bank: _mk(e, _bk),
                            after=[gt], guard=evac_guard)
                bank_last[bank_i] = ev
            else:
                bank_last[bank_i] = gt
            return gt, ev

        t_br = {}
        for i, nm in enumerate(("bq", "bk", "bv")):
            wnm = ("wq", "wk", "wv")[i]
            ev = None
            for s in range(2):
                emits = []
                for dc in range(ND):
                    emits.append(lambda e, bk, st, sp, _dc=dc, _s=s, _w=awb[wnm]: e.matmul(
                        bk[0:1, 0:512], bcolb[:, _dc:_dc + 1],
                        _w[:, _dc * D + _s * 512:_dc * D + _s * 512 + 512],
                        start=st, stop=False))
                emits.append(lambda e, bk, st, sp, _s=s, _r=rows[nm]: e.matmul(
                    bk[0:1, 0:512], onescol[0:1, 0:1], _r[0:1, _s * 512:_s * 512 + 512],
                    start=False, stop=True))
                _, ev = mm_group(6, emits,
                                 extra_after=[t_aw[wnm], t_bcolb, t_onescol, io_rows[nm]],
                                 evac=lambda e, bk, _s=s, _o=rows[nm]: e.copy(
                                     _o[0:1, _s * 512:_s * 512 + 512], bk[0:1, 0:512]))
            t_br[nm] = ev

        t_fold = {}
        for i, wnm in enumerate(("wq", "wk", "wv")):
            last = None
            for dc in range(ND):
                last = b.emit("vector", lambda e, _dc=dc, _w=awb[wnm]: e.tensor_scalar(
                    out=_w[:, _dc * D:(_dc + 1) * D], in0=_w[:, _dc * D:(_dc + 1) * D],
                    scalar1=gcolv[:, _dc:_dc + 1], scalar2=None, op0=ALU.mult),
                    after=[t_br[("bq", "bk", "bv")[i]], t_gcol, t_aw[wnm]])
            t_fold[wnm] = last

        tap("brow_bq", rows["bq"][:], [t_br["bq"]], [1, D], BF16)
        tap("wq_fold", awb["wq"][:, 0:2048], [t_fold["wq"]], [128, 2048], BF16)
        def transpose8(src_bf, dst_bf, after, banks=(6, 7)):
            evt = []
            for half in range(2):
                emits = []
                for j in range(4):
                    emits.append(lambda e, bk, st, sp, _j=j, _dc=half * 4 + j: e.matmul(
                        bk[:, _j * 128:(_j + 1) * 128],
                        src_bf[:, _dc * 128:(_dc + 1) * 128], identb[:],
                        start=st, stop=sp))
                _, ev = mm_group(banks[half], emits,
                                 extra_after=list(after) + [t_ident],
                                 evac=lambda e, bk, _h=half: e.copy(
                                     dst_bf[:, _h * 512:(_h + 1) * 512], bk[:, 0:512]))
                evt.append(ev)
            return evt

        def proj(dst_bf, srcT, wname, brow_t, brow_ap, after, banks=(0, 1), scale=None,
                 dst_f32=None):
            evt = []
            for s in range(2):
                emits = []
                for dc in range(ND):
                    emits.append(lambda e, bk, st, sp, _dc=dc, _s=s: e.matmul(
                        bk[:], srcT[:, _dc * 128:(_dc + 1) * 128],
                        awb[wname][:, _dc * D + _s * 512:_dc * D + _s * 512 + 512],
                        start=st, stop=False))
                emits.append(lambda e, bk, st, sp, _s=s: e.matmul(
                    bk[:], onescol[0:1, :], brow_ap[0:1, _s * 512:_s * 512 + 512],
                    start=False, stop=True))
                if scale is None:
                    evac = lambda e, bk, _s=s: e.copy(
                        dst_bf[:, _s * 512:(_s + 1) * 512], bk[:])
                else:
                    evac = lambda e, bk, _s=s: e.mul(
                        dst_f32[:, _s * 512:(_s + 1) * 512], bk[:], scale)
                _, ev = mm_group(banks[s], emits,
                                 extra_after=list(after) + [t_aw[wname], brow_t, t_onescol],
                                 evac=evac)
                evt.append(ev)
            return evt

        pi = {}
        x2_ticks = []
        t_sum_last = t_sq_last = None
        for c in range(NT):
            xc0 = xres[:, c * D:(c + 1) * D]
            t_sum_last = b.emit("vector", lambda e, _c=c, _x=xc0: e.tensor_reduce(
                out=sumc[:, _c:_c + 1], in_=_x, axis=AX.X, op=ALU.add), after=[t_x])
            t_sq_last = b.emit("scalar", lambda e, _c=c, _x=xc0: e.activation(
                out=tmps[:], in_=_x, func=AF.Square, bias=0.0, scale=1.0,
                accum_out=sqc[:, _c:_c + 1]), after=[t_x], guard=[t_sq_last])
        t_mu = b.emit("vector", lambda e: e.tensor_scalar(
            out=muc[:], in0=sumc[:], scalar1=1.0 / D, scalar2=None, op0=ALU.mult),
            guard=[t_sum_last])
        t_msq = b.emit("vector", lambda e: e.tensor_tensor(
            out=vp16[:], in0=muc[:], in1=muc[:], op=ALU.mult), guard=[t_mu])
        t_va = b.emit("vector", lambda e: e.scalar_tensor_tensor(
            out=vp16[:], in0=sqc[:], scalar=1.0 / D, in1=vp16[:],
            op0=ALU.mult, op1=ALU.subtract), after=[t_sq_last], guard=[t_msq])
        t_vp = b.emit("vector", lambda e: e.tensor_scalar(
            out=vp16[:], in0=vp16[:], scalar1=1.0 + EPS, scalar2=EPS * EPS,
            op0=ALU.mult, op1=ALU.add), guard=[t_va])
        t_ln = b.emit("scalar", lambda e: e.activation(
            out=rsc[:], in_=vp16[:], func=AF.Ln, bias=0.0, scale=1.0), after=[t_vp])
        t_rs = b.emit("scalar", lambda e: e.activation(
            out=rsc[:], in_=rsc[:], func=AF.Exp, bias=0.0, scale=-0.5), guard=[t_ln])
        front = [None] * NT
        backs = {}

        nticks = {}

        def F1V(c):
            xc = xres[:, c * D:(c + 1) * D]
            nticks[c] = b.emit("vector", lambda e, _c=c, _x=xc: e.tensor_scalar(
                out=rot["nb"][:], in0=_x, scalar1=muc[:, _c:_c + 1],
                scalar2=rsc[:, _c:_c + 1], op0=ALU.subtract, op1=ALU.mult),
                after=[t_rs, pi.get("nb")], guard=[t_mu])

        def F1T(c):
            r = c % 2
            t_n = nticks[c]
            t_nT = transpose8(rot["nb"], rot["nTb"], [t_n, pi.get("nTb")])
            t_q = proj(qbufs[r], rot["nTb"], "wq", t_br["bq"], rows["bq"],
                       t_nT + [t_fold["wq"], pi.get(("qb", r))], banks=(0, 1))
            t_k = proj(kbufs[r], rot["nTb"], "wk", t_br["bk"], rows["bk"],
                       t_nT + [t_fold["wk"], pi.get(("kb", r))], banks=(2, 3))
            t_v = proj(vbufs[r], rot["nTb"], "wv", t_br["bv"], rows["bv"],
                       t_nT + [t_fold["wv"], pi.get(("vb", r))], banks=(4, 5))
            pi["nb"] = t_nT[1]
            pi["nTb"] = t_v[1]
            front[c] = [t_q, t_k, t_v, None]
            if c == 0:
                tap("nb", rot["nb"][:], [t_n], [128, D], BF16)
                tap("nTb", rot["nTb"][:], t_nT, [128, D], BF16)
                tap("qb", qbufs[0][:], t_q, [128, D], BF16)
                tap("kb", kbufs[0][:], t_k, [128, D], BF16)
                tap("vb", vbufs[0][:], t_v, [128, D], BF16)

        def F2(c):
            r = c % 2
            qasb = qas[:, r * D:(r + 1) * D]
            t_q = front[c][0]
            t_qT = transpose8(qbufs[r], rot["qTb"], t_q + [pi.get("qTb")])
            t_qa = proj(None, rot["qTb"], "wqa", io_rows["bqa"], rows["bqa"],
                        t_qT + [pi.get(("qas", r))], banks=(0, 1),
                        scale=1.0 / 32.0, dst_f32=qasb)
            pi["qTb"] = t_qa[1]
            front[c][3] = t_qa
            if c == 0:
                tap("qas", qas[:, 0:D], t_qa, [128, D])

        def B1(c):
            r = c % 2
            qasb = qas[:, r * D:(r + 1) * D]
            t_qa = front[c][3]
            t_nm = b.emit("vector", lambda e, _q=qasb: e.tensor_reduce(
                out=nmr[:], in_=_q, axis=AX.X, op=ALU.max, negate=True),
                after=t_qa + [pi.get("nmr")])
            t_qw = b.emit("scalar", lambda e, _q=qasb: e.activation(
                out=rot["qwb"][:], in_=_q, func=AF.Exp, bias=nmr[:, 0:1], scale=1.0,
                accum_out=ssr[:, 0:1]), after=[t_nm, pi.get("qwb")])
            t_sr = b.emit("vector", lambda e: e.reciprocal(srr[:], ssr[:]), after=[t_qw])
            t_A = b.emit("vector", lambda e, _r=r: e.scalar_tensor_tensor(
                out=rot["Ac"][:], in0=rot["qwb"][:], scalar=srr[:, 0:1], in1=qbufs[_r][:],
                op0=ALU.mult, op1=ALU.mult), after=[pi.get("Ac")], guard=[t_sr])
            pi[("qb", r)] = t_A
            backs["qw"] = t_qw
            backs["A"] = t_A
            if c == 0:
                tap("qwb", rot["qwb"][:], [t_qw], [128, D], BF16)
                tap("Ac", rot["Ac"][:], [t_A], [128, D], BF16)

        def B2(c):
            r = c % 2
            qasb = qas[:, r * D:(r + 1) * D]
            t_q, t_k, t_v, t_qa = front[c]
            t_A = backs["A"]
            t_cs1 = []
            for s in range(2):
                emits = [
                    lambda e, bk, st, sp, _s=s: e.matmul(
                        bk[:], u128b[:], rot["Ac"][:, _s * 512:(_s + 1) * 512],
                        start=True, stop=False),
                    lambda e, bk, st, sp, _s=s: e.matmul(
                        bk[:], ones128b[:], rot["S1"][:, _s * 512:(_s + 1) * 512],
                        start=False, stop=True),
                ]
                _, ev = mm_group((0, 1)[s], emits,
                                 extra_after=[t_A, t_u128, t_ones, t_s1, pi.get("S1"),
                                              t_rt],
                                 evac=lambda e, bk, _s=s, _c=c: e.activation(
                                     out=rot["pqb"][:, _s * 512:(_s + 1) * 512], in_=bk[:],
                                     func=AF.Copy, bias=0.0, scale=rt[:, _c:_c + 1]))
                t_cs1.append(ev)
            t_S1 = b.emit("vector", lambda e: e.tensor_tensor(
                out=rot["S1"][:], in0=rot["S1"][:], in1=rot["Ac"][:], op=ALU.add),
                after=t_cs1)
            pi["S1"] = t_S1
            pi["Ac"] = t_S1
            t_mk = b.emit("vector", lambda e, _r=r: e.tensor_tensor(
                out=rot["mkb"][:], in0=rot["pqb"][:], in1=kbufs[_r][:], op=ALU.mult),
                after=t_cs1 + t_k + [pi.get("mkb")])
            t_mkT = transpose8(rot["mkb"], rot["mkTb"], [t_mk, pi.get("mkTb")])
            t_ka = proj(None, rot["mkTb"], "wka", io_rows["bka"], rows["bka"],
                        t_mkT + [backs["qw"]], banks=(2, 3), scale=1.0 / 32.0,
                        dst_f32=qasb)
            pi[("kb", r)] = t_mk
            pi["mkTb"] = t_ka[1]
            pi["pqb"] = t_mk
            backs["ka"] = t_ka
            backs["cs1"] = t_cs1
            backs["mk"] = t_mk
            if c == 0:
                tap("pqb", rot["pqb"][:], t_cs1, [128, D], BF16)
                tap("mkb", rot["mkb"][:], [t_mk], [128, D], BF16)

        def B3a(c):
            r = c % 2
            qasb = qas[:, r * D:(r + 1) * D]
            t_ka = backs["ka"]
            t_nm2 = b.emit("vector", lambda e, _q=qasb: e.tensor_reduce(
                out=nmr[:], in_=_q, axis=AX.X, op=ALU.max, negate=True),
                after=t_ka)
            t_kw = b.emit("scalar", lambda e, _q=qasb: e.activation(
                out=rot["qwb"][:], in_=_q, func=AF.Exp, bias=nmr[:, 0:1], scale=1.0,
                accum_out=ssr[:, 0:1]), after=[t_nm2, backs["A"]])
            t_sr2 = b.emit("vector", lambda e: e.reciprocal(srr[:], ssr[:]), after=[t_kw])
            t_B = b.emit("vector", lambda e: e.scalar_tensor_tensor(
                out=rot["Bc"][:], in0=rot["qwb"][:], scalar=srr[:, 0:1], in1=rot["mkb"][:],
                op0=ALU.mult, op1=ALU.mult), after=[pi.get("Bc")], guard=[t_sr2])
            pi[("qas", r)] = t_kw
            pi["mkb"] = t_B
            pi["qwb"] = t_B
            pi["nmr"] = t_kw
            backs["B"] = t_B
            if c == 0:
                tap("Bc", rot["Bc"][:], [t_B], [128, D], BF16)

        def B3b(c):
            r = c % 2
            xc = xres[:, c * D:(c + 1) * D]
            t_v = front[c][2]
            t_B = backs["B"]
            t_cs2 = []
            for s in range(2):
                emits = [
                    lambda e, bk, st, sp, _s=s: e.matmul(
                        bk[:], u128b[:], rot["Bc"][:, _s * 512:(_s + 1) * 512],
                        start=True, stop=False),
                    lambda e, bk, st, sp, _s=s: e.matmul(
                        bk[:], ones128b[:], rot["S2"][:, _s * 512:(_s + 1) * 512],
                        start=False, stop=True),
                ]
                _, ev = mm_group((4, 5)[s], emits,
                                 extra_after=[t_B, t_s2, pi.get("S2")],
                                 evac=lambda e, bk, _s=s, _c=c: e.activation(
                                     out=rot["pkb"][:, _s * 512:(_s + 1) * 512], in_=bk[:],
                                     func=AF.Copy, bias=0.0, scale=rt[:, _c:_c + 1]))
                t_cs2.append(ev)
            t_S2 = b.emit("vector", lambda e: e.tensor_tensor(
                out=rot["S2"][:], in0=rot["S2"][:], in1=rot["Bc"][:], op=ALU.add),
                after=t_cs2)
            pi["S2"] = t_S2
            t_tmp = b.emit("vector", lambda e, _r=r: e.tensor_tensor(
                out=tmps[:], in0=rot["pkb"][:], in1=vbufs[_r][:], op=ALU.mult),
                after=t_cs2 + t_v)
            t_x2 = b.emit("vector", lambda e, _x=xc: e.tensor_tensor(
                out=_x, in0=_x, in1=tmps[:], op=ALU.add), guard=[t_tmp])
            x2_ticks.append(t_x2)
            pi[("vb", r)] = t_tmp
            pi["pkb"] = t_tmp
            pi["Bc"] = t_S2
            if c == 0:
                tap("pkb", rot["pkb"][:], t_cs2, [128, D], BF16)
                tap("x2", xres[:, 0:D], [t_x2], [128, D])

        t_S1_last = [None]

        def B2_wrap(c):
            B2(c)
            t_S1_last[0] = pi["S1"]

        F1V(0)
        F1T(0)
        F2(0)
        for c in range(NT):
            B1(c)
            if c + 1 < NT:
                F1V(c + 1)
            if c >= 1:
                B3b(c - 1)
            if c + 1 < NT:
                F1T(c + 1)
            B2_wrap(c)
            B3a(c)
            if c + 1 < NT:
                F2(c + 1)
        B3b(NT - 1)

        attn_done = [("T", b.cnt["T"]), ("V", b.cnt["V"]), ("S", b.cnt["S"])]
        qas_g.__exit__(None, None, None)
        for g in reversed(list(rot_g.values())):
            g.__exit__(None, None, None)
        for g in reversed(list(awb_g.values())):
            g.__exit__(None, None, None)
        tmps_g.__exit__(None, None, None)
        for g in reversed(list(rows_g.values())):
            g.__exit__(None, None, None)

        ln2T = es.enter_context(nc.sbuf_tensor("ln2T", [128, ND * T], BF16))
        hid = es.enter_context(nc.sbuf_tensor("hid", [128, NDF * TH], BF16))
        wdb = es.enter_context(nc.sbuf_tensor("wdb", [128, NDF * D], BF16))
        wgq = [es.enter_context(nc.sbuf_tensor(f"wgq{i}", [128, ND * 128], BF16))
               for i in range(2)]
        wuq = [es.enter_context(nc.sbuf_tensor(f"wuq{i}", [128, ND * 128], BF16))
               for i in range(2)]
        ln2Te = es.enter_context(nc.sbuf_tensor("ln2Te", [128, ND * CAP], BF16))
        ye = xres
        idx16 = es.enter_context(nc.sbuf_tensor("idx16", [128, 8 * (CAP // 16)], I16))
        wgbuf = [es.enter_context(nc.sbuf_tensor(f"wgbuf{i}", [128, (CAP // 128) * 64],
                                                 F32)) for i in range(2)]
        gateb = es.enter_context(nc.sbuf_tensor("gateb", [128, ND * 8], BF16))
        ln2c = es.enter_context(nc.sbuf_tensor("ln2c", [128, D], BF16))
        sgb = es.enter_context(nc.sbuf_tensor("sgb", [128, TH], BF16))
        sqscrB = sgb
        LG = es.enter_context(nc.sbuf_tensor("LG", [128, NT * 8], F32))
        WG8 = es.enter_context(nc.sbuf_tensor("WG8", [128, NT * 64], F32))
        SEL = es.enter_context(nc.sbuf_tensor("SEL", [128, NT * 8], F32))
        identf = es.enter_context(nc.sbuf_tensor("identf", [128, 128], F32))
        ones16 = es.enter_context(nc.sbuf_tensor("ones16", [1, 16], F32))
        slotw = es.enter_context(nc.sbuf_tensor("slotw", [16, CAP // 16], F32))
        nfall = es.enter_context(nc.sbuf_tensor("nfall", [1, 8], U32))
        nff = es.enter_context(nc.sbuf_tensor("nff", [1, 1], F32))
        nfb = es.enter_context(nc.sbuf_tensor("nfb", [16, 1], F32))
        valtm = es.enter_context(nc.sbuf_tensor("valtm", [128, NT], F32))
        valw = es.enter_context(nc.sbuf_tensor("valw", [16, 128], F32))
        idxfw = es.enter_context(nc.sbuf_tensor("idxfw", [16, 128], F32))
        mk16 = es.enter_context(nc.sbuf_tensor("mk16", [16, CAP // 16], F32))
        idxm = es.enter_context(nc.sbuf_tensor("idxm", [16, CAP // 16], F32))
        t_identf = b.emit("vector", lambda e: e.tensor_scalar(
            out=identf[:], in0=Ft[:], scalar1=Pc[:, 0:1], scalar2=None,
            op0=ALU.is_equal), after=[t_iF, t_iP] + attn_done)
        t_ones16 = b.emit("gpsimd", lambda e: e.memset(ones16[:], 1.0),
                          after=attn_done)
        t_slotw = b.emit("gpsimd", lambda e: e.iota(
            slotw[:], [[16, CAP // 16]], base=0, channel_multiplier=1,
            allow_small_or_imprecise_dtypes=True), after=attn_done)
        t_wg8z = b.emit("gpsimd", lambda e: e.memset(WG8[:], 0.0),
                        after=attn_done)

        t_gw = b.emit("gpsimd", lambda e: e.dma_start(
            out=gateb[:], in_=dw["gatew"].rearrange("(c p) f -> p c f", p=128)),
            dma=True, after=attn_done)

        pc = {}
        gate_ticks = []
        lnw_ticks = []
        ln2_done = None
        t_sum_last = t_sq_last = None
        for c in range(NT):
            xc0 = xres[:, c * D:(c + 1) * D]
            t_sum_last = b.emit("vector", lambda e, _c=c, _x=xc0: e.tensor_reduce(
                out=sumc[:, _c:_c + 1], in_=_x, axis=AX.X, op=ALU.add),
                after=[x2_ticks[c]])
            t_sq_last = b.emit("scalar", lambda e, _c=c, _x=xc0: e.activation(
                out=sqscrB[:], in_=_x, func=AF.Square, bias=0.0, scale=1.0,
                accum_out=sqc[:, _c:_c + 1]), after=[x2_ticks[c]] + attn_done,
                guard=[t_sq_last])
        t_mu = b.emit("vector", lambda e: e.tensor_scalar(
            out=mu2c[:], in0=sumc[:], scalar1=1.0 / D, scalar2=None, op0=ALU.mult),
            guard=[t_sum_last])
        t_msq = b.emit("vector", lambda e: e.tensor_tensor(
            out=vp16[:], in0=mu2c[:], in1=mu2c[:], op=ALU.mult), guard=[t_mu])
        t_va = b.emit("vector", lambda e: e.scalar_tensor_tensor(
            out=vp16[:], in0=sqc[:], scalar=1.0 / D, in1=vp16[:],
            op0=ALU.mult, op1=ALU.subtract), after=[t_sq_last], guard=[t_msq])
        t_vp = b.emit("vector", lambda e: e.tensor_scalar(
            out=vp16[:], in0=vp16[:], scalar1=EPS, scalar2=None, op0=ALU.add),
            guard=[t_va])
        t_ln = b.emit("scalar", lambda e: e.activation(
            out=rs2c[:], in_=vp16[:], func=AF.Ln, bias=0.0, scale=1.0), after=[t_vp])
        t_rs = b.emit("scalar", lambda e: e.activation(
            out=rs2c[:], in_=rs2c[:], func=AF.Exp, bias=0.0, scale=-0.5), guard=[t_ln])
        for c in range(NT):
            xc = xres[:, c * D:(c + 1) * D]
            t_n = b.emit("vector", lambda e, _c=c, _x=xc: e.tensor_scalar(
                out=ln2c[:], in0=_x, scalar1=mu2c[:, _c:_c + 1],
                scalar2=rs2c[:, _c:_c + 1], op0=ALU.subtract, op1=ALU.mult),
                after=[t_rs, pc.get("ln2c"), pc.get("ln2c_evt")] + attn_done,
                guard=[t_mu])
            evt = []
            for half in range(2):
                emits = []
                for j in range(4):
                    emits.append(lambda e, bk, st, sp, _j=j, _dc=half * 4 + j: e.matmul(
                        bk[:, _j * 128:(_j + 1) * 128],
                        ln2c[:, _dc * 128:(_dc + 1) * 128], identb[:],
                        start=st, stop=sp))
                bank_i = (6, 7)[half]
                bank = pb[bank_i]
                gt = None
                n = len(emits)
                for i2, mk in enumerate(emits):
                    aft = [t_n, t_ident, bank_last[bank_i]] if i2 == 0 else ()
                    gt = b.emit("tensor",
                                lambda e, _mk=mk, _bk=bank, _st=(i2 == 0),
                                _sp=(i2 == n - 1): _mk(e, _bk, _st, _sp),
                                after=aft, inc=(i2 == n - 1))
                for j in range(4):
                    dc = half * 4 + j
                    evt.append(b.emit("scalar", lambda e, _dc=dc, _j=j, _bk=bank, _c=c: e.copy(
                        ln2T[:, _dc * T + _c * 128:_dc * T + (_c + 1) * 128],
                        _bk[:, _j * 128:(_j + 1) * 128]), after=[gt]))
                bank_last[bank_i] = evt[-1]
            t_lnw = b.emit("sync", lambda e, _c=c: e.dma_start(
                out=lnhbm.rearrange("(c p) f -> p c f", p=128)[:, _c:_c + 1, :],
                in_=ln2c[:]), dma=True, after=[t_n])
            lnw_ticks.append(t_lnw)
            pc["ln2c"] = t_lnw
            pc["ln2c_evt"] = evt[-1]
            emits = []
            for dc in range(ND):
                emits.append(lambda e, bk, st, sp, _dc=dc, _c=c: e.matmul(
                    bk[:, 0:8], ln2T[:, _dc * T + _c * 128:_dc * T + (_c + 1) * 128],
                    gateb[:, _dc * 8:(_dc + 1) * 8], start=st, stop=sp))
            _, t_lg = mm_group(0, emits, extra_after=[t_gw] + evt,
                               evac=lambda e, bk, _c=c: e.tensor_copy(
                                   LG[:, _c * 8:(_c + 1) * 8], bk[:, 0:8]),
                               evac_eng="vector")
            lgc = LG[:, c * 8:(c + 1) * 8]
            t_m1 = b.emit("vector", lambda e, _l=lgc: e.tensor_reduce(
                out=nm1c[:], in_=_l, axis=AX.X, op=ALU.max, negate=True),
                after=[pc.get("g_nm1")], guard=[t_lg])
            t_msk = b.emit("vector", lambda e, _l=lgc: e.tensor_scalar(
                out=selc[:], in0=_l, scalar1=nm1c[:, 0:1], scalar2=0.0,
                op0=ALU.add, op1=ALU.is_ge), guard=[t_m1], after=[pc.get("g_sel")])
            t_lp = b.emit("vector", lambda e, _l=lgc: e.scalar_tensor_tensor(
                out=selc[:], in0=selc[:], scalar=-1e30, in1=_l,
                op0=ALU.mult, op1=ALU.add), guard=[t_msk])
            t_m2 = b.emit("vector", lambda e: e.tensor_reduce(
                out=nm2c[:], in_=selc[:], axis=AX.X, op=ALU.max, negate=True),
                after=[pc.get("g_nm2")], guard=[t_lp])
            t_sel = b.emit("vector", lambda e, _l=lgc, _c=c: e.tensor_scalar(
                out=SEL[:, _c * 8:(_c + 1) * 8], in0=_l, scalar1=nm2c[:, 0:1],
                scalar2=0.0, op0=ALU.add, op1=ALU.is_ge), guard=[t_m2])
            t_E = b.emit("scalar", lambda e, _l=lgc, _c=c: e.activation(
                out=WG8[:, _c * 64:_c * 64 + 8], in_=_l, func=AF.Exp, bias=nm1c[:, 0:1],
                scale=1.0), after=[t_m1, t_lp, pc.get("g_E"), t_wg8z])
            t_ed = b.emit("scalar", lambda e: e.activation(
                out=edc[:], in_=nm2c[:], func=AF.Exp, bias=nm1c[:, 0:1], scale=-1.0),
                after=[t_m2, pc.get("g_ed")])
            t_den = b.emit("vector", lambda e: e.tensor_scalar(
                out=edc[:], in0=edc[:], scalar1=1.0, scalar2=None, op0=ALU.add),
                after=[t_ed])
            t_rd = b.emit("vector", lambda e: e.reciprocal(nm1c[:], edc[:]),
                          guard=[t_den])
            t_w = b.emit("vector", lambda e, _c=c: e.scalar_tensor_tensor(
                out=WG8[:, _c * 64:_c * 64 + 8], in0=WG8[:, _c * 64:_c * 64 + 8],
                scalar=nm1c[:, 0:1], in1=SEL[:, _c * 8:(_c + 1) * 8],
                op0=ALU.mult, op1=ALU.mult),
                after=[t_E], guard=[t_rd, t_sel])
            gate_ticks.append(t_w)
            if c == 0:
                tap("ln2c", ln2T[:, 0:128], [evt[0]], [128, 128], BF16)
                tap("LG", LG[:, 0:8], [t_lg], [128, 8])
                tap("WG", WG8[:, 0:8], [t_w], [128, 8])
            pc["g_nm1"] = t_w
            pc["g_sel"] = t_w
            pc["g_nm2"] = t_E
            pc["g_E"] = t_w
            pc["g_ed"] = t_w
            ln2_done = evt[-1]

        t_whw = b.emit("sync", lambda e: e.dma_start(
            out=whbm.rearrange("(c p) f -> p c f", p=128)[:, :, :],
            in_=WG8.rearrange("p (c f) -> p c f", f=64)[:, :, :]),
            dma=True, after=[gate_ticks[-1]])

        regcell = [None, None]

        def get_reg(e, i):
            if regcell[i] is None:
                regcell[i] = e.alloc_register(name=f"nfreg{i}")
            return regcell[i]

        route_ticks = []
        prev_tT = None
        for ei in range(8):
            tV1 = b.emit("vector", lambda e, _e=ei: e.tensor_tensor(
                out=valtm[:], in0=SEL.rearrange("p (c f) -> p c f", f=8)[:, :, _e:_e + 1],
                in1=tvals[:], op=ALU.mult), guard=[gate_ticks[-1]],
                after=[t_itv, prev_tT])
            tV2 = b.emit("vector", lambda e: e.tensor_scalar(
                out=valtm[:], in0=valtm[:], scalar1=1.0, scalar2=None,
                op0=ALU.subtract), guard=[tV1])
            tT = b.emit("tensor", lambda e: e.matmul(
                pb[6][0:16, 0:128], valtm[:], identf[:], start=True, stop=True),
                after=[tV2, t_identf, bank_last[6]])
            prev_tT = tT
            tS = b.emit("scalar", lambda e: e.copy(valw[:], pb[6][0:16, 0:128]),
                        after=[tT])
            bank_last[6] = tS
            tG = b.emit("gpsimd", lambda e, _e=ei: e.sparse_gather(
                out=idxfw[:], in_=valw[:], num_found=nfall[0:1, _e:_e + 1]),
                after=[tS])
            tV3 = b.emit("vector", lambda e, _e=ei: e.tensor_copy(
                nff[:], nfall[0:1, _e:_e + 1]), after=[tG])
            tT2 = b.emit("tensor", lambda e: e.matmul(
                pb[6][0:16, 0:1], ones16[0:1, :], nff[0:1, 0:1], start=True,
                stop=True),
                after=[tV3, t_ones16, bank_last[6]])
            tS2 = b.emit("scalar", lambda e: e.copy(nfb[:], pb[6][0:16, 0:1]),
                         after=[tT2])
            bank_last[6] = tS2
            tV4 = b.emit("vector", lambda e: e.tensor_scalar(
                out=mk16[:], in0=slotw[:], scalar1=nfb[0:16, 0:1], scalar2=None,
                op0=ALU.is_lt), after=[tS2, t_slotw])
            tV5 = b.emit("vector", lambda e: e.tensor_scalar(
                out=idxm[:], in0=idxfw[0:16, 0:CAP // 16], scalar1=1.0,
                scalar2=None, op0=ALU.add), after=[tG])
            tV6 = b.emit("vector", lambda e: e.tensor_tensor(
                out=idxm[:], in0=idxm[:], in1=mk16[:], op=ALU.mult),
                guard=[tV5, tV4])
            tV7 = b.emit("vector", lambda e: e.tensor_scalar(
                out=idxm[:], in0=idxm[:], scalar1=1.0, scalar2=None,
                op0=ALU.subtract), guard=[tV6])
            tV8 = b.emit("vector", lambda e, _e=ei: e.tensor_copy(
                idx16[0:16, _e * (CAP // 16):(_e + 1) * (CAP // 16)], idxm[:]),
                guard=[tV7])
            route_ticks.append(tV8)
        tap("nfall", nfall[0:1, :], [route_ticks[-1]], [1, 8], U32)
        tap("idx16", idx16[0:16, :], [route_ticks[-1]], [16, 8 * (CAP // 16)], I16)
        tap("SELt", SEL[:, 0:128], [gate_ticks[-1]], [128, 128])
        t_repl = None
        for rr in range(1, 8):
            t_repl = b.emit("gpsimd", lambda e, _r=rr: e.dma_start(
                out=idx16[_r * 16:(_r + 1) * 16, :], in_=idx16[0:16, :]),
                dma=True, after=[route_ticks[-1]])

        t_wd_lastuse = None
        ring_lastuse = {}
        ring_tick = {}
        hid_ticks = {}
        t_hid_lastuse = None
        last_ev = None

        t_store = None
        t_scatter = None
        pend_scatter = [None]
        a_done_hist = {}
        wg_lastuse = [None, None]
        t_gathe = {}
        NS = CAP // 128

        ln2te_bufs = (ln2T[:, 0:ND * CAP], ln2Te[:])

        def emit_gathers(e_next, a_done_tick):
            buf = e_next % 2
            ri = e_next % 2
            gbuf = ln2te_bufs[e_next % 2]
            t_rl = b.emit("gpsimd", lambda e, _e=e_next, _i=ri: e.reg_load(
                get_reg(e, _i), nfall[0:1, _e - 1:_e]),
                after=[route_ticks[e_next - 1], t_repl], inc=False)
            t_mz = b.emit("gpsimd", lambda e, _g=gbuf: e.memzero(_g),
                          after=[a_done_tick])
            t_ga = b.emit("gpsimd", lambda e, _e=e_next, _i=ri, _g=gbuf: e.dma_gather(
                out_ap=_g.rearrange("p (c f) -> p c f", f=CAP)[:, :, :],
                in_ap=lnhbm[:, :],
                idxs_ap=idx16[:, (_e - 1) * (CAP // 16):_e * (CAP // 16)],
                num_idxs=CAP, num_idxs_reg=regcell[_i], elem_size=D,
                transpose=True),
                dma=True, after=[lnw_ticks[-1], a_done_tick, t_mz])
            t_mz2 = b.emit("gpsimd", lambda e, _b=buf: e.memzero(wgbuf[_b][:]),
                           after=[wg_lastuse[buf]])
            t_gw2 = b.emit("gpsimd", lambda e, _e=e_next, _b=buf, _i=ri: e.dma_gather(
                out_ap=wgbuf[_b].rearrange("p (c f) -> p c f", f=64)[:, :, :],
                in_ap=whbm[:, :],
                idxs_ap=idx16[:, (_e - 1) * (CAP // 16):_e * (CAP // 16)],
                num_idxs=CAP, num_idxs_reg=regcell[_i], elem_size=64,
                transpose=False),
                dma=True, after=[t_whw, wg_lastuse[buf], t_mz2])
            t_gathe[e_next] = (t_ga, t_gw2, buf, gbuf)

        def _e_r(e):
            return e - 1

        def flush_scatter():
            nonlocal t_scatter
            if pend_scatter[0] is None:
                return
            _e, _yev = pend_scatter[0]
            pend_scatter[0] = None
            t_scatter = b.emit("gpsimd", lambda e, _ee=_e: e.dma_scatter_add(
                out_ap=dy[:, :],
                in_ap=ye[:, 0:NS * D].rearrange("p (c f) -> p c f", f=D)[:, :, :],
                idxs_ap=idx16[:, (_ee - 1) * (CAP // 16):_ee * (CAP // 16)],
                num_idxs=CAP, num_idxs_reg=regcell[_ee % 2], elem_size=D),
                dma=True,
                after=[t_store, t_scatter, _yev, t_repl])

        for e_i in range(1 + N_ROUTED):
            routed = e_i >= 1
            nth = 1 if routed else NTH
            t_wd_load = None
            if routed:
                t_ga, t_gw2, wbuf, gbuf_e = t_gathe[e_i]
            for th in range(nth):
                for dck in range(NDF):
                    buf = dck % 2
                    ring_tick[("g", buf)] = b.emit("gpsimd",
                        lambda e, _e=e_i, _d=dck, _b=buf: e.dma_start(
                            out=wgq[_b][:],
                            in_=dw[f"wg{_e}"][:, _d * 128:(_d + 1) * 128]
                                .rearrange("(c p) f -> p c f", p=128)),
                        dma=True, after=[ring_lastuse.get(("g", buf))] + attn_done)
                    ring_tick[("u", buf)] = b.emit("gpsimd",
                        lambda e, _e=e_i, _d=dck, _b=buf: e.dma_start(
                            out=wuq[_b][:],
                            in_=dw[f"wu{_e}"][:, _d * 128:(_d + 1) * 128]
                                .rearrange("(c p) f -> p c f", p=128)),
                        dma=True, after=[ring_lastuse.get(("u", buf))] + attn_done)
                    if t_wd_load is None:
                        t_wd_load = b.emit("gpsimd", lambda e, _e=e_i: e.dma_start(
                            out=wdb[:], in_=dw[f"wd{_e}"].rearrange(
                                "(c p) f -> p c f", p=128)),
                            dma=True, after=[t_wd_lastuse] + attn_done)
                        flush_scatter()
                    par = dck % 2
                    gbi = (0, 1) if par == 0 else (4, 5)
                    ubi = (2, 3) if par == 0 else (6, 7)
                    g_t, u_t = [], []
                    for s in range(2):
                        for kind, bis, wq_ in (("g", gbi, wgq[buf]),
                                               ("u", ubi, wuq[buf])):
                            emits = []
                            for dc in range(ND):
                                if routed:
                                    emits.append(lambda e, bk, st, sp, _dc=dc,
                                                 _s=s, _w=wq_, _g=gbuf_e: e.matmul(
                                        bk[:],
                                        _w[:, _dc * 128:(_dc + 1) * 128],
                                        _g[:, _dc * CAP + _s * 512:
                                           _dc * CAP + (_s + 1) * 512],
                                        start=st, stop=sp))
                                else:
                                    emits.append(lambda e, bk, st, sp, _dc=dc,
                                                 _s=s, _th=th, _w=wq_: e.matmul(
                                        bk[:],
                                        _w[:, _dc * 128:(_dc + 1) * 128],
                                        ln2T[:, _dc * T + _th * TH + _s * 512:
                                             _dc * T + _th * TH + (_s + 1) * 512],
                                        start=st, stop=sp))
                            aft = [ring_tick[(kind, buf)], ln2_done,
                                   gate_ticks[-1]]
                            if routed:
                                aft.append(t_ga)
                            gt, _ = mm_group(bis[s], emits, extra_after=aft)
                            (g_t if kind == "g" else u_t).append(gt)
                    for s in range(2):
                        t_sg = b.emit("scalar", lambda e, _s=s, _bk=pb[gbi[s]]: e.activation(
                            out=sgb[:, _s * 512:(_s + 1) * 512], in_=_bk[:],
                            func=AF.Silu, bias=0.0, scale=1.0),
                            after=[g_t[s], pc.get("sgb%d" % s)])
                        bank_last[gbi[s]] = t_sg
                        t_h = b.emit("vector", lambda e, _s=s, _dck=dck,
                                     _bk=pb[ubi[s]]: e.tensor_tensor(
                            out=hid[:, _dck * TH + _s * 512:_dck * TH + (_s + 1) * 512],
                            in0=sgb[:, _s * 512:(_s + 1) * 512], in1=_bk[:],
                            op=ALU.mult),
                            after=[u_t[s], t_sg, t_hid_lastuse])
                        bank_last[ubi[s]] = t_h
                        pc["sgb%d" % s] = t_h
                        hid_ticks[(dck, s)] = t_h
                    ring_lastuse[("g", buf)] = g_t[-1]
                    ring_lastuse[("u", buf)] = u_t[-1]
                a_done_hist[e_i] = u_t[-1]
                if th == nth - 1 and e_i < N_ROUTED:
                    prev_use = a_done_hist.get(e_i - 1)
                    if e_i + 1 >= 2 and prev_use is None:
                        prev_use = u_t[-1]
                    emit_gathers(e_i + 1, prev_use if e_i >= 1 else u_t[-1])
                down_last = []
                ye_evs = []
                for tc in range(8):
                    gc = th * 8 + tc
                    bi = (0, 1) if tc % 2 == 0 else (2, 3)
                    for s in range(2):
                        emits = []
                        for dck in range(NDF):
                            emits.append(lambda e, bk, st, sp, _dck=dck, _tc=tc,
                                         _s=s: e.matmul(
                                bk[:],
                                hid[:, _dck * TH + _tc * 128:_dck * TH + (_tc + 1) * 128],
                                wdb[:, _dck * D + _s * 512:_dck * D + (_s + 1) * 512],
                                start=st, stop=sp))
                        hid_all = [hid_ticks[(d2, s2)] for d2 in range(NDF)
                                   for s2 in range(2)] if (tc == 0 and s == 0) else []
                        if not routed:
                            xslice = xres[:, gc * D + s * 512:gc * D + (s + 1) * 512]
                            evac = lambda e, bk, _x=xslice: e.tensor_tensor(
                                out=_x, in0=bk[:], in1=_x, op=ALU.add)
                            eg = []
                        else:
                            ysl = ye[:, tc * D + s * 512:tc * D + (s + 1) * 512]
                            evac = lambda e, bk, _y=ysl, _tc=tc, _e=e_i, _b=wbuf:                                 e.tensor_scalar(
                                    out=_y, in0=bk[:],
                                    scalar1=wgbuf[_b][:, _tc * 64 + (_e - 1):
                                                      _tc * 64 + _e],
                                    scalar2=None, op0=ALU.mult)
                            eg = [t_gw2, t_scatter, t_store]
                        gt, ev = mm_group(bi[s], emits,
                                          extra_after=[t_wd_load] + hid_all,
                                          evac=evac, evac_eng="vector",
                                          evac_guard=eg)
                        down_last.append(gt)
                        ye_evs.append(ev)
                        last_ev = ev
                t_hid_lastuse = down_last[-1]
                if routed:
                    wg_lastuse[wbuf] = ye_evs[-1]
            t_wd_lastuse = down_last[-1]
            if not routed:
                t_store = b.emit("sync", lambda e: e.dma_start(
                    out=dy.rearrange("(c p) f -> p c f", p=128), in_=xres[:]),
                    dma=True, after=[last_ev])
            else:
                pend_scatter[0] = (e_i, ye_evs[-1])
                if e_i == N_ROUTED:
                    flush_scatter()

        def final_wait(e, _f=t_scatter, _g=t_store):
            e.wait_ge(b.sems[_g[0]], _g[1])
            if _f is not None:
                e.wait_ge(b.sems[_f[0]], _f[1])
        b.ev["sync"].append(final_wait)

        block = es.enter_context(nc.Block())

        @block.tensor
        def _(t):
            for fn in b.ev["tensor"]:
                fn(t)

        @block.vector
        def _(v):
            for fn in b.ev["vector"]:
                fn(v)

        @block.scalar
        def _(s):
            for fn in b.ev["scalar"]:
                fn(s)

        @block.gpsimd
        def _(g):
            for fn in b.ev["gpsimd"]:
                fn(g)

        @block.sync
        def _(sy):
            for fn in b.ev["sync"]:
                fn(sy)

    import bass_rust as _bass_rust
    from concourse.library_config import all_libraries, standard
    _mask = {}
    for _lib in all_libraries:
        for _it in _lib.instructions:
            _mask[_it] = _mask.get(_it, 0) | (1 << _lib.index)
    _bass_rust.insert_library_loads(nc, _mask, len(all_libraries), standard.index)
    mybir.codegen_inst_isa_subclasses(nc)
    return nc


def _in_map(inputs, bi):
    g = np.ascontiguousarray
    m = {
        "x": inputs["L0_attn_hidden_states"][bi],
        "wq": inputs["L0_attn_Wq"], "wk": inputs["L0_attn_Wk"],
        "wv": inputs["L0_attn_Wv"], "wqa": inputs["L0_attn_Wqa"],
        "wka": inputs["L0_attn_Wka"],
        "bq": inputs["L0_attn_bq"].reshape(1, D),
        "bk": inputs["L0_attn_bk"].reshape(1, D),
        "bv": inputs["L0_attn_bv"].reshape(1, D),
        "bqa": inputs["L0_attn_bqa"].reshape(1, D),
        "bka": inputs["L0_attn_bka"].reshape(1, D),
        "gcol": inputs["L0_attn_gamma"].reshape(ND, 128).T,
        "bcol": inputs["L0_attn_beta"].reshape(ND, 128).T,
        "gatew": inputs["L0_ffn_gate_w"],
        "wg0": inputs["L0_ffn_ws_gate"], "wu0": inputs["L0_ffn_ws_up"],
        "wd0": inputs["L0_ffn_ws_down"],
    }
    for e in range(8):
        m[f"wg{e + 1}"] = inputs["L0_ffn_w_gate"][e]
        m[f"wu{e + 1}"] = inputs["L0_ffn_w_up"][e]
        m[f"wd{e + 1}"] = inputs["L0_ffn_w_down"][e]
    return {k: g(np.asarray(v, dtype=np.float32)) for k, v in m.items()}


def kernel(**inputs):
    nc = build_program()
    in_maps = [_in_map(inputs, bi) for bi in range(8)]
    res = run_bass_kernel_spmd(nc, in_maps, core_ids=list(range(8)))
    out = np.stack([res.results[i]["y"] for i in range(8)], axis=0)
    return np.ascontiguousarray(out.astype(np.float32))


# version 16 (final)
# speedup vs baseline: 1.0509x; 1.0334x over previous
import numpy as np
from contextlib import ExitStack
import concourse.bass as bass
import concourse.mybir as mybir
from concourse.bass_utils import run_bass_kernel_spmd

F32 = mybir.dt.float32
BF16 = mybir.dt.bfloat16
I16 = mybir.dt.int16
U32 = mybir.dt.uint32
AF = mybir.ActivationFunctionType
ALU = mybir.AluOpType
AX = mybir.AxisListType

T, NT = 2048, 16
CAP = 1024
NS = CAP // 128
D, ND = 1024, 8
DFF, NDF = 2048, 16
TH, NTH = 1024, 2
EPS = 1e-5

ENGINES = ("tensor", "vector", "scalar", "gpsimd", "sync")
ESEM = {"tensor": "T", "vector": "V", "scalar": "S", "gpsimd": "G", "sync": "Y"}


class Bld:

    NWD = 12
    NIO = 4

    def __init__(self):
        self.ev = {e: [] for e in ENGINES}
        self.cnt = {}
        self.lastw = {e: {} for e in ENGINES}
        self.sems = {}
        self.wd_rr = 0
        self.io_rr = 0

    def emit(self, eng, mk, after=(), guard=(), dma=False, inc=True):
        lane_prev = None
        if dma:
            if eng == "gpsimd":
                sem = f"WD{self.wd_rr % self.NWD}"
                self.wd_rr += 1
            else:
                sem = f"IO{self.io_rr % self.NIO}"
                self.io_rr += 1
            if self.cnt.get(sem, 0) > 0:
                lane_prev = (sem, self.cnt[sem])
        else:
            sem = ESEM[eng]
        want = {}
        if lane_prev is not None:
            want[lane_prev[0]] = lane_prev[1]
        for t in after:
            if t is None:
                continue
            s, v = t
            if want.get(s, 0) < v:
                want[s] = v
        for t in guard:
            if t is None:
                continue
            s, v = t
            if want.get(s, 0) < v:
                want[s] = v
        lw = self.lastw[eng]
        ws = []
        for s, v in want.items():
            if lw.get(s, 0) >= v:
                continue
            lw[s] = v
            ws.append((s, v))
        tick = None
        n = 16 if dma else 1
        if inc:
            self.cnt[sem] = self.cnt.get(sem, 0) + n
            tick = (sem, self.cnt[sem])
        B = self

        def fn(e, _ws=tuple(ws), _mk=mk, _sem=sem, _n=n, _inc=inc):
            for s, v in _ws:
                e.wait_ge(B.sems[s], v)
            ins = _mk(e)
            if _inc:
                ins.then_inc(B.sems[_sem], _n)
        self.ev[eng].append(fn)
        return tick


DEBUG = False
N_ROUTED = 8


def build_program():
    nc = bass.Bass("TRN2", target_bir_lowering=False)
    b = Bld()

    def tap(name, ap, after, shape, dtype=F32):
        if not DEBUG:
            return
        d = nc.dram_tensor("dbg_" + name, shape, dtype, kind="ExternalOutput").ap()
        b.emit("sync", lambda e: e.dma_start(out=d[:, :], in_=ap), dma=True,
               after=list(after))
    bank_last = [None] * 8

    dx = nc.dram_tensor("x", [T, D], F32, kind="ExternalInput").ap()
    dy = nc.dram_tensor("y", [T, D], F32, kind="ExternalOutput").ap()
    dw = {}
    for nm in ("wq", "wk", "wv", "wqa", "wka"):
        dw[nm] = nc.dram_tensor(nm, [D, D], F32, kind="ExternalInput").ap()
    for nm in ("bq", "bk", "bv", "bqa", "bka"):
        dw[nm] = nc.dram_tensor(nm, [1, D], F32, kind="ExternalInput").ap()
    dw["gcol"] = nc.dram_tensor("gcol", [128, ND], F32, kind="ExternalInput").ap()
    dw["bcol"] = nc.dram_tensor("bcol", [128, ND], F32, kind="ExternalInput").ap()
    dw["gatew"] = nc.dram_tensor("gatew", [D, 8], F32, kind="ExternalInput").ap()
    lnhbm = nc.dram_tensor("lnhbm", [T, D], BF16, kind="Internal").ap()
    whbm = nc.dram_tensor("whbm", [T, 64], F32, kind="Internal").ap()
    for e in range(9):
        dw[f"wg{e}"] = nc.dram_tensor(f"wg{e}", [D, DFF], F32, kind="ExternalInput").ap()
        dw[f"wu{e}"] = nc.dram_tensor(f"wu{e}", [D, DFF], F32, kind="ExternalInput").ap()
        dw[f"wd{e}"] = nc.dram_tensor(f"wd{e}", [DFF, D], F32, kind="ExternalInput").ap()

    es = ExitStack()

    def sb(name, shape, dt):
        return es.enter_context(nc.sbuf_tensor(name, shape, dt))

    with es:
        xres = sb("xres", [128, NT * D], F32)
        Ft = sb("Ft", [128, 128], F32)
        Pc = sb("Pc", [128, 1], F32)
        identb = sb("identb", [128, 128], BF16)
        u128b = sb("u128b", [128, 128], BF16)
        ones128b = sb("ones128b", [128, 128], BF16)
        onescol = sb("onescol", [1, 128], BF16)
        tvals = sb("tvals", [128, NT], F32)
        rt = sb("rt", [128, NT], F32)
        gcolv = sb("gcolv", [128, ND], F32)
        bcolv = sb("bcolv", [128, ND], F32)
        bcolb = sb("bcolb", [128, ND], BF16)
        sumc = sb("sumc", [128, NT], F32)
        sqc = sb("sqc", [128, NT], F32)
        muc = sb("muc", [128, NT], F32)
        rsc = sb("rsc", [128, NT], F32)
        mu2c = sb("mu2c", [128, NT], F32)
        rs2c = sb("rs2c", [128, NT], F32)
        vp16 = sb("vp16", [128, NT], F32)
        nm1c = sb("nm1c", [128, 1], F32)
        nm2c = sb("nm2c", [128, 1], F32)
        edc = sb("edc", [128, 1], F32)
        selc = sb("selc", [128, 8], F32)
        nmr = sb("nmr", [128, 1], F32)
        ssr = sb("ssr", [128, 1], F32)
        srr = sb("srr", [128, 1], F32)

        rows_g = {nm: nc.sbuf_tensor("r_" + nm, [1, D], BF16)
                  for nm in ("bq", "bk", "bv", "bqa", "bka")}
        rows = {nm: g.__enter__() for nm, g in rows_g.items()}
        tmps_g = nc.sbuf_tensor("tmps", [128, D], BF16)
        tmps = tmps_g.__enter__()
        awb_g = {nm: nc.sbuf_tensor("b_" + nm, [128, ND * D], BF16) for nm in
                 ("wq", "wk", "wv", "wqa", "wka")}
        awb = {nm: g.__enter__() for nm, g in awb_g.items()}
        rot_g = {}
        rot = {}
        for nm in ("nb", "nTb", "qb", "qb1", "qTb", "kb", "kb1", "vb", "vb1",
                   "mkb", "mkTb", "pqb", "pkb", "Ac", "Bc", "qwb", "S1", "S2"):
            rot_g[nm] = nc.sbuf_tensor("t_" + nm, [128, D], BF16)
            rot[nm] = rot_g[nm].__enter__()
        qas_g = nc.sbuf_tensor("qas", [128, 2 * D], F32)
        qas = qas_g.__enter__()
        qbufs = (rot["qb"], rot["qb1"])
        kbufs = (rot["kb"], rot["kb1"])
        vbufs = (rot["vb"], rot["vb1"])

        pb = [es.enter_context(nc.psum_tensor(f"pb{i}", [128, 512], F32)) for i in range(8)]

        for s in (["T", "V", "S", "G", "Y"]
                  + [f"WD{i}" for i in range(Bld.NWD)]
                  + [f"IO{i}" for i in range(Bld.NIO)]):
            b.sems[s] = es.enter_context(nc.semaphore("sem" + s))

        t_x = b.emit("sync", lambda e: e.dma_start(
            out=xres[:], in_=dx.rearrange("(c p) f -> p c f", p=128)), dma=True)
        io_rows = {}
        for nm in ("bq", "bk", "bv", "bqa", "bka"):
            io_rows[nm] = b.emit("gpsimd", lambda e, a=rows[nm], d=dw[nm]: e.dma_start(
                out=a[:], in_=d[:, :]), dma=True)
        t_gcol = b.emit("sync", lambda e: e.dma_start(out=gcolv[:], in_=dw["gcol"][:, :]), dma=True)
        t_bcol = b.emit("sync", lambda e: e.dma_start(out=bcolv[:], in_=dw["bcol"][:, :]), dma=True)

        t_iF = b.emit("gpsimd", lambda e: e.iota(
            Ft[:], [[1, 128]], base=0, channel_multiplier=0,
            allow_small_or_imprecise_dtypes=True))
        t_iP = b.emit("gpsimd", lambda e: e.iota(
            Pc[:], [[0, 1]], base=0, channel_multiplier=1,
            allow_small_or_imprecise_dtypes=True))
        t_itv = b.emit("gpsimd", lambda e: e.iota(
            tvals[:], [[128, NT]], base=1, channel_multiplier=1,
            allow_small_or_imprecise_dtypes=True))
        t_ones = b.emit("gpsimd", lambda e: e.memset(ones128b[:], 1.0))
        t_onescol = b.emit("gpsimd", lambda e: e.memset(onescol[:], 1.0))
        t_s1 = b.emit("gpsimd", lambda e: e.memset(rot["S1"][:], 0.0))
        t_s2 = b.emit("gpsimd", lambda e: e.memset(rot["S2"][:], 0.0))

        t_ident = b.emit("vector", lambda e: e.tensor_scalar(
            out=identb[:], in0=Ft[:], scalar1=Pc[:, 0:1], scalar2=None,
            op0=ALU.is_equal), after=[t_iF, t_iP])
        t_u128 = b.emit("vector", lambda e: e.tensor_scalar(
            out=u128b[:], in0=Ft[:], scalar1=Pc[:, 0:1], scalar2=None,
            op0=ALU.is_ge), after=[t_iF, t_iP])
        t_rt = b.emit("vector", lambda e: e.reciprocal(rt[:], tvals[:]), after=[t_itv])
        t_bcolb = b.emit("vector", lambda e: e.tensor_copy(bcolb[:], bcolv[:]), after=[t_bcol])

        t_aw = {}
        for nm in ("wq", "wk", "wv", "wqa", "wka"):
            t_aw[nm] = b.emit("gpsimd", lambda e, a=awb[nm], d=dw[nm]: e.dma_start(
                out=a[:], in_=d.rearrange("(c p) f -> p c f", p=128)), dma=True)

        def mm_group(bank_i, emits, extra_after=(), evac=None, evac_eng="scalar",
                     evac_guard=()):
            bank = pb[bank_i]
            n = len(emits)
            gt = None
            for i, mk in enumerate(emits):
                aft = (list(extra_after) + [bank_last[bank_i]]) if i == 0 else ()
                gt = b.emit("tensor",
                            lambda e, _mk=mk, _bk=bank, _st=(i == 0), _sp=(i == n - 1): _mk(
                                e, _bk, _st, _sp),
                            after=aft, inc=(i == n - 1))
            ev = None
            if evac is not None:
                ev = b.emit(evac_eng, lambda e, _mk=evac, _bk=bank: _mk(e, _bk),
                            after=[gt], guard=evac_guard)
                bank_last[bank_i] = ev
            else:
                bank_last[bank_i] = gt
            return gt, ev

        t_br = {}
        for i, nm in enumerate(("bq", "bk", "bv")):
            wnm = ("wq", "wk", "wv")[i]
            ev = None
            for s in range(2):
                emits = []
                for dc in range(ND):
                    emits.append(lambda e, bk, st, sp, _dc=dc, _s=s, _w=awb[wnm]: e.matmul(
                        bk[0:1, 0:512], bcolb[:, _dc:_dc + 1],
                        _w[:, _dc * D + _s * 512:_dc * D + _s * 512 + 512],
                        start=st, stop=False))
                emits.append(lambda e, bk, st, sp, _s=s, _r=rows[nm]: e.matmul(
                    bk[0:1, 0:512], onescol[0:1, 0:1], _r[0:1, _s * 512:_s * 512 + 512],
                    start=False, stop=True))
                _, ev = mm_group(6, emits,
                                 extra_after=[t_aw[wnm], t_bcolb, t_onescol, io_rows[nm]],
                                 evac=lambda e, bk, _s=s, _o=rows[nm]: e.copy(
                                     _o[0:1, _s * 512:_s * 512 + 512], bk[0:1, 0:512]))
            t_br[nm] = ev

        t_fold = {}
        for i, wnm in enumerate(("wq", "wk", "wv")):
            last = None
            for dc in range(ND):
                last = b.emit("vector", lambda e, _dc=dc, _w=awb[wnm]: e.tensor_scalar(
                    out=_w[:, _dc * D:(_dc + 1) * D], in0=_w[:, _dc * D:(_dc + 1) * D],
                    scalar1=gcolv[:, _dc:_dc + 1], scalar2=None, op0=ALU.mult),
                    after=[t_br[("bq", "bk", "bv")[i]], t_gcol, t_aw[wnm]])
            t_fold[wnm] = last

        tap("brow_bq", rows["bq"][:], [t_br["bq"]], [1, D], BF16)
        tap("wq_fold", awb["wq"][:, 0:2048], [t_fold["wq"]], [128, 2048], BF16)
        def transpose8(src_bf, dst_bf, after, banks=(6, 7)):
            evt = []
            for half in range(2):
                emits = []
                for j in range(4):
                    emits.append(lambda e, bk, st, sp, _j=j, _dc=half * 4 + j: e.matmul(
                        bk[:, _j * 128:(_j + 1) * 128],
                        src_bf[:, _dc * 128:(_dc + 1) * 128], identb[:],
                        start=st, stop=sp))
                _, ev = mm_group(banks[half], emits,
                                 extra_after=list(after) + [t_ident],
                                 evac=lambda e, bk, _h=half: e.copy(
                                     dst_bf[:, _h * 512:(_h + 1) * 512], bk[:, 0:512]))
                evt.append(ev)
            return evt

        def proj(dst_bf, srcT, wname, brow_t, brow_ap, after, banks=(0, 1), scale=None,
                 dst_f32=None):
            evt = []
            for s in range(2):
                emits = []
                for dc in range(ND):
                    emits.append(lambda e, bk, st, sp, _dc=dc, _s=s: e.matmul(
                        bk[:], srcT[:, _dc * 128:(_dc + 1) * 128],
                        awb[wname][:, _dc * D + _s * 512:_dc * D + _s * 512 + 512],
                        start=st, stop=False))
                emits.append(lambda e, bk, st, sp, _s=s: e.matmul(
                    bk[:], onescol[0:1, :], brow_ap[0:1, _s * 512:_s * 512 + 512],
                    start=False, stop=True))
                if scale is None:
                    evac = lambda e, bk, _s=s: e.copy(
                        dst_bf[:, _s * 512:(_s + 1) * 512], bk[:])
                else:
                    evac = lambda e, bk, _s=s: e.mul(
                        dst_f32[:, _s * 512:(_s + 1) * 512], bk[:], scale)
                _, ev = mm_group(banks[s], emits,
                                 extra_after=list(after) + [t_aw[wname], brow_t, t_onescol],
                                 evac=evac)
                evt.append(ev)
            return evt

        pi = {}
        x2_ticks = []
        t_sum_last = t_sq_last = None
        for c in range(NT):
            xc0 = xres[:, c * D:(c + 1) * D]
            t_sum_last = b.emit("vector", lambda e, _c=c, _x=xc0: e.tensor_reduce(
                out=sumc[:, _c:_c + 1], in_=_x, axis=AX.X, op=ALU.add), after=[t_x])
            t_sq_last = b.emit("scalar", lambda e, _c=c, _x=xc0: e.activation(
                out=tmps[:], in_=_x, func=AF.Square, bias=0.0, scale=1.0,
                accum_out=sqc[:, _c:_c + 1]), after=[t_x], guard=[t_sq_last])
        t_mu = b.emit("vector", lambda e: e.tensor_scalar(
            out=muc[:], in0=sumc[:], scalar1=1.0 / D, scalar2=None, op0=ALU.mult),
            guard=[t_sum_last])
        t_msq = b.emit("vector", lambda e: e.tensor_tensor(
            out=vp16[:], in0=muc[:], in1=muc[:], op=ALU.mult), guard=[t_mu])
        t_va = b.emit("vector", lambda e: e.scalar_tensor_tensor(
            out=vp16[:], in0=sqc[:], scalar=1.0 / D, in1=vp16[:],
            op0=ALU.mult, op1=ALU.subtract), after=[t_sq_last], guard=[t_msq])
        t_vp = b.emit("vector", lambda e: e.tensor_scalar(
            out=vp16[:], in0=vp16[:], scalar1=1.0 + EPS, scalar2=EPS * EPS,
            op0=ALU.mult, op1=ALU.add), guard=[t_va])
        t_ln = b.emit("scalar", lambda e: e.activation(
            out=rsc[:], in_=vp16[:], func=AF.Ln, bias=0.0, scale=1.0), after=[t_vp])
        t_rs = b.emit("scalar", lambda e: e.activation(
            out=rsc[:], in_=rsc[:], func=AF.Exp, bias=0.0, scale=-0.5), guard=[t_ln])
        front = [None] * NT
        backs = {}

        nticks = {}

        def F1V(c):
            xc = xres[:, c * D:(c + 1) * D]
            nticks[c] = b.emit("vector", lambda e, _c=c, _x=xc: e.tensor_scalar(
                out=rot["nb"][:], in0=_x, scalar1=muc[:, _c:_c + 1],
                scalar2=rsc[:, _c:_c + 1], op0=ALU.subtract, op1=ALU.mult),
                after=[t_rs, pi.get("nb")], guard=[t_mu])

        def F1T(c):
            r = c % 2
            t_n = nticks[c]
            t_nT = transpose8(rot["nb"], rot["nTb"], [t_n, pi.get("nTb")])
            t_q = proj(qbufs[r], rot["nTb"], "wq", t_br["bq"], rows["bq"],
                       t_nT + [t_fold["wq"], pi.get(("qb", r))], banks=(0, 1))
            t_k = proj(kbufs[r], rot["nTb"], "wk", t_br["bk"], rows["bk"],
                       t_nT + [t_fold["wk"], pi.get(("kb", r))], banks=(2, 3))
            t_v = proj(vbufs[r], rot["nTb"], "wv", t_br["bv"], rows["bv"],
                       t_nT + [t_fold["wv"], pi.get(("vb", r))], banks=(4, 5))
            pi["nb"] = t_nT[1]
            pi["nTb"] = t_v[1]
            front[c] = [t_q, t_k, t_v, None]
            if c == 0:
                tap("nb", rot["nb"][:], [t_n], [128, D], BF16)
                tap("nTb", rot["nTb"][:], t_nT, [128, D], BF16)
                tap("qb", qbufs[0][:], t_q, [128, D], BF16)
                tap("kb", kbufs[0][:], t_k, [128, D], BF16)
                tap("vb", vbufs[0][:], t_v, [128, D], BF16)

        def F2(c):
            r = c % 2
            qasb = qas[:, r * D:(r + 1) * D]
            t_q = front[c][0]
            t_qT = transpose8(qbufs[r], rot["qTb"], t_q + [pi.get("qTb")])
            t_qa = proj(None, rot["qTb"], "wqa", io_rows["bqa"], rows["bqa"],
                        t_qT + [pi.get(("qas", r))], banks=(0, 1),
                        scale=1.0 / 32.0, dst_f32=qasb)
            pi["qTb"] = t_qa[1]
            front[c][3] = t_qa
            if c == 0:
                tap("qas", qas[:, 0:D], t_qa, [128, D])

        def B1(c):
            r = c % 2
            qasb = qas[:, r * D:(r + 1) * D]
            t_qa = front[c][3]
            t_nm = b.emit("vector", lambda e, _q=qasb: e.tensor_reduce(
                out=nmr[:], in_=_q, axis=AX.X, op=ALU.max, negate=True),
                after=t_qa + [pi.get("nmr")])
            t_qw = b.emit("scalar", lambda e, _q=qasb: e.activation(
                out=rot["qwb"][:], in_=_q, func=AF.Exp, bias=nmr[:, 0:1], scale=1.0,
                accum_out=ssr[:, 0:1]), after=[t_nm, pi.get("qwb")])
            t_sr = b.emit("vector", lambda e: e.reciprocal(srr[:], ssr[:]), after=[t_qw])
            t_A = b.emit("vector", lambda e, _r=r: e.scalar_tensor_tensor(
                out=rot["Ac"][:], in0=rot["qwb"][:], scalar=srr[:, 0:1], in1=qbufs[_r][:],
                op0=ALU.mult, op1=ALU.mult), after=[pi.get("Ac")], guard=[t_sr])
            pi[("qb", r)] = t_A
            backs["qw"] = t_qw
            backs["A"] = t_A
            if c == 0:
                tap("qwb", rot["qwb"][:], [t_qw], [128, D], BF16)
                tap("Ac", rot["Ac"][:], [t_A], [128, D], BF16)

        def B2(c):
            r = c % 2
            qasb = qas[:, r * D:(r + 1) * D]
            t_q, t_k, t_v, t_qa = front[c]
            t_A = backs["A"]
            t_cs1 = []
            for s in range(2):
                emits = [
                    lambda e, bk, st, sp, _s=s: e.matmul(
                        bk[:], u128b[:], rot["Ac"][:, _s * 512:(_s + 1) * 512],
                        start=True, stop=False),
                    lambda e, bk, st, sp, _s=s: e.matmul(
                        bk[:], ones128b[:], rot["S1"][:, _s * 512:(_s + 1) * 512],
                        start=False, stop=True),
                ]
                _, ev = mm_group((0, 1)[s], emits,
                                 extra_after=[t_A, t_u128, t_ones, t_s1, pi.get("S1"),
                                              t_rt],
                                 evac=lambda e, bk, _s=s, _c=c: e.activation(
                                     out=rot["pqb"][:, _s * 512:(_s + 1) * 512], in_=bk[:],
                                     func=AF.Copy, bias=0.0, scale=rt[:, _c:_c + 1]))
                t_cs1.append(ev)
            t_S1 = b.emit("vector", lambda e: e.tensor_tensor(
                out=rot["S1"][:], in0=rot["S1"][:], in1=rot["Ac"][:], op=ALU.add),
                after=t_cs1)
            pi["S1"] = t_S1
            pi["Ac"] = t_S1
            t_mk = b.emit("vector", lambda e, _r=r: e.tensor_tensor(
                out=rot["mkb"][:], in0=rot["pqb"][:], in1=kbufs[_r][:], op=ALU.mult),
                after=t_cs1 + t_k + [pi.get("mkb")])
            t_mkT = transpose8(rot["mkb"], rot["mkTb"], [t_mk, pi.get("mkTb")])
            t_ka = proj(None, rot["mkTb"], "wka", io_rows["bka"], rows["bka"],
                        t_mkT + [backs["qw"]], banks=(2, 3), scale=1.0 / 32.0,
                        dst_f32=qasb)
            pi[("kb", r)] = t_mk
            pi["mkTb"] = t_ka[1]
            pi["pqb"] = t_mk
            backs["ka"] = t_ka
            backs["cs1"] = t_cs1
            backs["mk"] = t_mk
            if c == 0:
                tap("pqb", rot["pqb"][:], t_cs1, [128, D], BF16)
                tap("mkb", rot["mkb"][:], [t_mk], [128, D], BF16)

        def B3a(c):
            r = c % 2
            qasb = qas[:, r * D:(r + 1) * D]
            t_ka = backs["ka"]
            t_nm2 = b.emit("vector", lambda e, _q=qasb: e.tensor_reduce(
                out=nmr[:], in_=_q, axis=AX.X, op=ALU.max, negate=True),
                after=t_ka)
            t_kw = b.emit("scalar", lambda e, _q=qasb: e.activation(
                out=rot["qwb"][:], in_=_q, func=AF.Exp, bias=nmr[:, 0:1], scale=1.0,
                accum_out=ssr[:, 0:1]), after=[t_nm2, backs["A"]])
            t_sr2 = b.emit("vector", lambda e: e.reciprocal(srr[:], ssr[:]), after=[t_kw])
            t_B = b.emit("vector", lambda e: e.scalar_tensor_tensor(
                out=rot["Bc"][:], in0=rot["qwb"][:], scalar=srr[:, 0:1], in1=rot["mkb"][:],
                op0=ALU.mult, op1=ALU.mult), after=[pi.get("Bc")], guard=[t_sr2])
            pi[("qas", r)] = t_kw
            pi["mkb"] = t_B
            pi["qwb"] = t_B
            pi["nmr"] = t_kw
            backs["B"] = t_B
            if c == 0:
                tap("Bc", rot["Bc"][:], [t_B], [128, D], BF16)

        def B3b(c):
            r = c % 2
            xc = xres[:, c * D:(c + 1) * D]
            t_v = front[c][2]
            t_B = backs["B"]
            t_cs2 = []
            for s in range(2):
                emits = [
                    lambda e, bk, st, sp, _s=s: e.matmul(
                        bk[:], u128b[:], rot["Bc"][:, _s * 512:(_s + 1) * 512],
                        start=True, stop=False),
                    lambda e, bk, st, sp, _s=s: e.matmul(
                        bk[:], ones128b[:], rot["S2"][:, _s * 512:(_s + 1) * 512],
                        start=False, stop=True),
                ]
                _, ev = mm_group((4, 5)[s], emits,
                                 extra_after=[t_B, t_s2, pi.get("S2")],
                                 evac=lambda e, bk, _s=s, _c=c: e.activation(
                                     out=rot["pkb"][:, _s * 512:(_s + 1) * 512], in_=bk[:],
                                     func=AF.Copy, bias=0.0, scale=rt[:, _c:_c + 1]))
                t_cs2.append(ev)
            t_S2 = b.emit("vector", lambda e: e.tensor_tensor(
                out=rot["S2"][:], in0=rot["S2"][:], in1=rot["Bc"][:], op=ALU.add),
                after=t_cs2)
            pi["S2"] = t_S2
            t_tmp = b.emit("vector", lambda e, _r=r: e.tensor_tensor(
                out=tmps[:], in0=rot["pkb"][:], in1=vbufs[_r][:], op=ALU.mult),
                after=t_cs2 + t_v)
            t_x2 = b.emit("vector", lambda e, _x=xc: e.tensor_tensor(
                out=_x, in0=_x, in1=tmps[:], op=ALU.add), guard=[t_tmp])
            x2_ticks.append(t_x2)
            pi[("vb", r)] = t_tmp
            pi["pkb"] = t_tmp
            pi["Bc"] = t_S2
            if c == 0:
                tap("pkb", rot["pkb"][:], t_cs2, [128, D], BF16)
                tap("x2", xres[:, 0:D], [t_x2], [128, D])

        t_S1_last = [None]

        def B2_wrap(c):
            B2(c)
            t_S1_last[0] = pi["S1"]

        F1V(0)
        F1T(0)
        F2(0)
        for c in range(NT):
            B1(c)
            if c + 1 < NT:
                F1V(c + 1)
            if c >= 1:
                B3b(c - 1)
            if c + 1 < NT:
                F1T(c + 1)
            B2_wrap(c)
            B3a(c)
            if c + 1 < NT:
                F2(c + 1)
        B3b(NT - 1)

        attn_done = [("T", b.cnt["T"]), ("V", b.cnt["V"]), ("S", b.cnt["S"])]
        qas_g.__exit__(None, None, None)
        for g in reversed(list(rot_g.values())):
            g.__exit__(None, None, None)
        for g in reversed(list(awb_g.values())):
            g.__exit__(None, None, None)
        tmps_g.__exit__(None, None, None)
        for g in reversed(list(rows_g.values())):
            g.__exit__(None, None, None)

        ln2T = es.enter_context(nc.sbuf_tensor("ln2T", [128, ND * T], BF16))
        hid = es.enter_context(nc.sbuf_tensor("hid", [128, NDF * TH], BF16))
        wdb = es.enter_context(nc.sbuf_tensor("wdb", [128, NDF * D], BF16))
        wgq = [es.enter_context(nc.sbuf_tensor(f"wgq{i}", [128, ND * 128], BF16))
               for i in range(2)]
        wuq = [es.enter_context(nc.sbuf_tensor(f"wuq{i}", [128, ND * 128], BF16))
               for i in range(2)]
        ln2Te = es.enter_context(nc.sbuf_tensor("ln2Te", [128, ND * CAP], BF16))
        ye_bufs = (xres[:, 0:NS * D], xres[:, NS * D:2 * NS * D])
        idx16 = es.enter_context(nc.sbuf_tensor("idx16", [128, 8 * (CAP // 16)], I16))
        wgbuf = [es.enter_context(nc.sbuf_tensor(f"wgbuf{i}", [128, (CAP // 128) * 64],
                                                 F32)) for i in range(2)]
        gateb = es.enter_context(nc.sbuf_tensor("gateb", [128, ND * 8], BF16))
        ln2c = es.enter_context(nc.sbuf_tensor("ln2c", [128, D], BF16))
        sgb = es.enter_context(nc.sbuf_tensor("sgb", [128, TH], BF16))
        sqscrB = sgb
        LG = es.enter_context(nc.sbuf_tensor("LG", [128, NT * 8], F32))
        WG8 = es.enter_context(nc.sbuf_tensor("WG8", [128, NT * 64], F32))
        SEL = es.enter_context(nc.sbuf_tensor("SEL", [128, NT * 8], F32))
        identf = es.enter_context(nc.sbuf_tensor("identf", [128, 128], F32))
        ones16 = es.enter_context(nc.sbuf_tensor("ones16", [1, 16], F32))
        slotw = es.enter_context(nc.sbuf_tensor("slotw", [16, CAP // 16], F32))
        nfall = es.enter_context(nc.sbuf_tensor("nfall", [1, 8], U32))
        nff = es.enter_context(nc.sbuf_tensor("nff", [1, 1], F32))
        nfb = es.enter_context(nc.sbuf_tensor("nfb", [16, 1], F32))
        valtm = es.enter_context(nc.sbuf_tensor("valtm", [128, NT], F32))
        valw = es.enter_context(nc.sbuf_tensor("valw", [16, 128], F32))
        idxfw = es.enter_context(nc.sbuf_tensor("idxfw", [16, 128], F32))
        mk16 = es.enter_context(nc.sbuf_tensor("mk16", [16, CAP // 16], F32))
        idxm = es.enter_context(nc.sbuf_tensor("idxm", [16, CAP // 16], F32))
        t_identf = b.emit("vector", lambda e: e.tensor_scalar(
            out=identf[:], in0=Ft[:], scalar1=Pc[:, 0:1], scalar2=None,
            op0=ALU.is_equal), after=[t_iF, t_iP] + attn_done)
        t_ones16 = b.emit("gpsimd", lambda e: e.memset(ones16[:], 1.0),
                          after=attn_done)
        t_slotw = b.emit("gpsimd", lambda e: e.iota(
            slotw[:], [[16, CAP // 16]], base=0, channel_multiplier=1,
            allow_small_or_imprecise_dtypes=True), after=attn_done)
        t_wg8z = b.emit("gpsimd", lambda e: e.memset(WG8[:], 0.0),
                        after=attn_done)

        t_gw = b.emit("gpsimd", lambda e: e.dma_start(
            out=gateb[:], in_=dw["gatew"].rearrange("(c p) f -> p c f", p=128)),
            dma=True, after=attn_done)

        pc = {}
        gate_ticks = []
        lnw_ticks = []
        ln2_done = None
        t_sum_last = t_sq_last = None
        for c in range(NT):
            xc0 = xres[:, c * D:(c + 1) * D]
            t_sum_last = b.emit("vector", lambda e, _c=c, _x=xc0: e.tensor_reduce(
                out=sumc[:, _c:_c + 1], in_=_x, axis=AX.X, op=ALU.add),
                after=[x2_ticks[c]])
            t_sq_last = b.emit("scalar", lambda e, _c=c, _x=xc0: e.activation(
                out=sqscrB[:], in_=_x, func=AF.Square, bias=0.0, scale=1.0,
                accum_out=sqc[:, _c:_c + 1]), after=[x2_ticks[c]] + attn_done,
                guard=[t_sq_last])
        t_mu = b.emit("vector", lambda e: e.tensor_scalar(
            out=mu2c[:], in0=sumc[:], scalar1=1.0 / D, scalar2=None, op0=ALU.mult),
            guard=[t_sum_last])
        t_msq = b.emit("vector", lambda e: e.tensor_tensor(
            out=vp16[:], in0=mu2c[:], in1=mu2c[:], op=ALU.mult), guard=[t_mu])
        t_va = b.emit("vector", lambda e: e.scalar_tensor_tensor(
            out=vp16[:], in0=sqc[:], scalar=1.0 / D, in1=vp16[:],
            op0=ALU.mult, op1=ALU.subtract), after=[t_sq_last], guard=[t_msq])
        t_vp = b.emit("vector", lambda e: e.tensor_scalar(
            out=vp16[:], in0=vp16[:], scalar1=EPS, scalar2=None, op0=ALU.add),
            guard=[t_va])
        t_ln = b.emit("scalar", lambda e: e.activation(
            out=rs2c[:], in_=vp16[:], func=AF.Ln, bias=0.0, scale=1.0), after=[t_vp])
        t_rs = b.emit("scalar", lambda e: e.activation(
            out=rs2c[:], in_=rs2c[:], func=AF.Exp, bias=0.0, scale=-0.5), guard=[t_ln])
        for c in range(NT):
            xc = xres[:, c * D:(c + 1) * D]
            t_n = b.emit("vector", lambda e, _c=c, _x=xc: e.tensor_scalar(
                out=ln2c[:], in0=_x, scalar1=mu2c[:, _c:_c + 1],
                scalar2=rs2c[:, _c:_c + 1], op0=ALU.subtract, op1=ALU.mult),
                after=[t_rs, pc.get("ln2c"), pc.get("ln2c_evt")] + attn_done,
                guard=[t_mu])
            evt = []
            for half in range(2):
                emits = []
                for j in range(4):
                    emits.append(lambda e, bk, st, sp, _j=j, _dc=half * 4 + j: e.matmul(
                        bk[:, _j * 128:(_j + 1) * 128],
                        ln2c[:, _dc * 128:(_dc + 1) * 128], identb[:],
                        start=st, stop=sp))
                bank_i = (6, 7)[half]
                bank = pb[bank_i]
                gt = None
                n = len(emits)
                for i2, mk in enumerate(emits):
                    aft = [t_n, t_ident, bank_last[bank_i]] if i2 == 0 else ()
                    gt = b.emit("tensor",
                                lambda e, _mk=mk, _bk=bank, _st=(i2 == 0),
                                _sp=(i2 == n - 1): _mk(e, _bk, _st, _sp),
                                after=aft, inc=(i2 == n - 1))
                for j in range(4):
                    dc = half * 4 + j
                    evt.append(b.emit("scalar", lambda e, _dc=dc, _j=j, _bk=bank, _c=c: e.copy(
                        ln2T[:, _dc * T + _c * 128:_dc * T + (_c + 1) * 128],
                        _bk[:, _j * 128:(_j + 1) * 128]), after=[gt]))
                bank_last[bank_i] = evt[-1]
            t_lnw = b.emit("sync", lambda e, _c=c: e.dma_start(
                out=lnhbm.rearrange("(c p) f -> p c f", p=128)[:, _c:_c + 1, :],
                in_=ln2c[:]), dma=True, after=[t_n])
            lnw_ticks.append(t_lnw)
            pc["ln2c"] = t_lnw
            pc["ln2c_evt"] = evt[-1]
            emits = []
            for dc in range(ND):
                emits.append(lambda e, bk, st, sp, _dc=dc, _c=c: e.matmul(
                    bk[:, 0:8], ln2T[:, _dc * T + _c * 128:_dc * T + (_c + 1) * 128],
                    gateb[:, _dc * 8:(_dc + 1) * 8], start=st, stop=sp))
            _, t_lg = mm_group(0, emits, extra_after=[t_gw] + evt,
                               evac=lambda e, bk, _c=c: e.tensor_copy(
                                   LG[:, _c * 8:(_c + 1) * 8], bk[:, 0:8]),
                               evac_eng="vector")
            lgc = LG[:, c * 8:(c + 1) * 8]
            t_m1 = b.emit("vector", lambda e, _l=lgc: e.tensor_reduce(
                out=nm1c[:], in_=_l, axis=AX.X, op=ALU.max, negate=True),
                after=[pc.get("g_nm1")], guard=[t_lg])
            t_msk = b.emit("vector", lambda e, _l=lgc: e.tensor_scalar(
                out=selc[:], in0=_l, scalar1=nm1c[:, 0:1], scalar2=0.0,
                op0=ALU.add, op1=ALU.is_ge), guard=[t_m1], after=[pc.get("g_sel")])
            t_lp = b.emit("vector", lambda e, _l=lgc: e.scalar_tensor_tensor(
                out=selc[:], in0=selc[:], scalar=-1e30, in1=_l,
                op0=ALU.mult, op1=ALU.add), guard=[t_msk])
            t_m2 = b.emit("vector", lambda e: e.tensor_reduce(
                out=nm2c[:], in_=selc[:], axis=AX.X, op=ALU.max, negate=True),
                after=[pc.get("g_nm2")], guard=[t_lp])
            t_sel = b.emit("vector", lambda e, _l=lgc, _c=c: e.tensor_scalar(
                out=SEL[:, _c * 8:(_c + 1) * 8], in0=_l, scalar1=nm2c[:, 0:1],
                scalar2=0.0, op0=ALU.add, op1=ALU.is_ge), guard=[t_m2])
            t_E = b.emit("scalar", lambda e, _l=lgc, _c=c: e.activation(
                out=WG8[:, _c * 64:_c * 64 + 8], in_=_l, func=AF.Exp, bias=nm1c[:, 0:1],
                scale=1.0), after=[t_m1, t_lp, pc.get("g_E"), t_wg8z])
            t_ed = b.emit("scalar", lambda e: e.activation(
                out=edc[:], in_=nm2c[:], func=AF.Exp, bias=nm1c[:, 0:1], scale=-1.0),
                after=[t_m2, pc.get("g_ed")])
            t_den = b.emit("vector", lambda e: e.tensor_scalar(
                out=edc[:], in0=edc[:], scalar1=1.0, scalar2=None, op0=ALU.add),
                after=[t_ed])
            t_rd = b.emit("vector", lambda e: e.reciprocal(nm1c[:], edc[:]),
                          guard=[t_den])
            t_w = b.emit("vector", lambda e, _c=c: e.scalar_tensor_tensor(
                out=WG8[:, _c * 64:_c * 64 + 8], in0=WG8[:, _c * 64:_c * 64 + 8],
                scalar=nm1c[:, 0:1], in1=SEL[:, _c * 8:(_c + 1) * 8],
                op0=ALU.mult, op1=ALU.mult),
                after=[t_E], guard=[t_rd, t_sel])
            gate_ticks.append(t_w)
            if c == 0:
                tap("ln2c", ln2T[:, 0:128], [evt[0]], [128, 128], BF16)
                tap("LG", LG[:, 0:8], [t_lg], [128, 8])
                tap("WG", WG8[:, 0:8], [t_w], [128, 8])
            pc["g_nm1"] = t_w
            pc["g_sel"] = t_w
            pc["g_nm2"] = t_E
            pc["g_E"] = t_w
            pc["g_ed"] = t_w
            ln2_done = evt[-1]

        t_whw = b.emit("sync", lambda e: e.dma_start(
            out=whbm.rearrange("(c p) f -> p c f", p=128)[:, :, :],
            in_=WG8.rearrange("p (c f) -> p c f", f=64)[:, :, :]),
            dma=True, after=[gate_ticks[-1]])

        regcell = [None, None]

        def get_reg(e, i):
            if regcell[i] is None:
                regcell[i] = e.alloc_register(name=f"nfreg{i}")
            return regcell[i]

        route_ticks = []
        prev_tT = None
        for ei in range(8):
            tV1 = b.emit("vector", lambda e, _e=ei: e.tensor_tensor(
                out=valtm[:], in0=SEL.rearrange("p (c f) -> p c f", f=8)[:, :, _e:_e + 1],
                in1=tvals[:], op=ALU.mult), guard=[gate_ticks[-1]],
                after=[t_itv, prev_tT])
            tV2 = b.emit("vector", lambda e: e.tensor_scalar(
                out=valtm[:], in0=valtm[:], scalar1=1.0, scalar2=None,
                op0=ALU.subtract), guard=[tV1])
            tT = b.emit("tensor", lambda e: e.matmul(
                pb[6][0:16, 0:128], valtm[:], identf[:], start=True, stop=True),
                after=[tV2, t_identf, bank_last[6]])
            prev_tT = tT
            tS = b.emit("scalar", lambda e: e.copy(valw[:], pb[6][0:16, 0:128]),
                        after=[tT])
            bank_last[6] = tS
            tG = b.emit("gpsimd", lambda e, _e=ei: e.sparse_gather(
                out=idxfw[:], in_=valw[:], num_found=nfall[0:1, _e:_e + 1]),
                after=[tS])
            tV3 = b.emit("vector", lambda e, _e=ei: e.tensor_copy(
                nff[:], nfall[0:1, _e:_e + 1]), after=[tG])
            tT2 = b.emit("tensor", lambda e: e.matmul(
                pb[6][0:16, 0:1], ones16[0:1, :], nff[0:1, 0:1], start=True,
                stop=True),
                after=[tV3, t_ones16, bank_last[6]])
            tS2 = b.emit("scalar", lambda e: e.copy(nfb[:], pb[6][0:16, 0:1]),
                         after=[tT2])
            bank_last[6] = tS2
            tV4 = b.emit("vector", lambda e: e.tensor_scalar(
                out=mk16[:], in0=slotw[:], scalar1=nfb[0:16, 0:1], scalar2=None,
                op0=ALU.is_lt), after=[tS2, t_slotw])
            tV5 = b.emit("vector", lambda e: e.tensor_scalar(
                out=idxm[:], in0=idxfw[0:16, 0:CAP // 16], scalar1=1.0,
                scalar2=None, op0=ALU.add), after=[tG])
            tV6 = b.emit("vector", lambda e: e.tensor_tensor(
                out=idxm[:], in0=idxm[:], in1=mk16[:], op=ALU.mult),
                guard=[tV5, tV4])
            tV7 = b.emit("vector", lambda e: e.tensor_scalar(
                out=idxm[:], in0=idxm[:], scalar1=1.0, scalar2=None,
                op0=ALU.subtract), guard=[tV6])
            tV8 = b.emit("vector", lambda e, _e=ei: e.tensor_copy(
                idx16[0:16, _e * (CAP // 16):(_e + 1) * (CAP // 16)], idxm[:]),
                guard=[tV7])
            route_ticks.append(tV8)
        tap("nfall", nfall[0:1, :], [route_ticks[-1]], [1, 8], U32)
        tap("idx16", idx16[0:16, :], [route_ticks[-1]], [16, 8 * (CAP // 16)], I16)
        tap("SELt", SEL[:, 0:128], [gate_ticks[-1]], [128, 128])
        t_repl = None
        for rr in range(1, 8):
            t_repl = b.emit("gpsimd", lambda e, _r=rr: e.dma_start(
                out=idx16[_r * 16:(_r + 1) * 16, :], in_=idx16[0:16, :]),
                dma=True, after=[route_ticks[-1]])

        t_wd_lastuse = None
        ring_lastuse = {}
        ring_tick = {}
        hid_ticks = {}
        t_hid_lastuse = None
        last_ev = None

        t_store = None
        t_scatter = None
        pend_scatter = [None]
        a_done_hist = {}
        wg_lastuse = [None, None]
        t_gathe = {}

        ln2te_bufs = (ln2T[:, 0:ND * CAP], ln2Te[:])

        def emit_gathers(e_next, a_done_tick):
            buf = e_next % 2
            ri = e_next % 2
            gbuf = ln2te_bufs[e_next % 2]
            t_rl = b.emit("gpsimd", lambda e, _e=e_next, _i=ri: e.reg_load(
                get_reg(e, _i), nfall[0:1, _e - 1:_e]),
                after=[route_ticks[e_next - 1], t_repl], inc=False)
            t_mz = b.emit("gpsimd", lambda e, _g=gbuf: e.memzero(_g),
                          after=[a_done_tick])
            t_ga = b.emit("gpsimd", lambda e, _e=e_next, _i=ri, _g=gbuf: e.dma_gather(
                out_ap=_g.rearrange("p (c f) -> p c f", f=CAP)[:, :, :],
                in_ap=lnhbm[:, :],
                idxs_ap=idx16[:, (_e - 1) * (CAP // 16):_e * (CAP // 16)],
                num_idxs=CAP, num_idxs_reg=regcell[_i], elem_size=D,
                transpose=True),
                dma=True, after=[lnw_ticks[-1], a_done_tick, t_mz])
            t_mz2 = b.emit("gpsimd", lambda e, _b=buf: e.memzero(wgbuf[_b][:]),
                           after=[wg_lastuse[buf]])
            t_gw2 = b.emit("gpsimd", lambda e, _e=e_next, _b=buf, _i=ri: e.dma_gather(
                out_ap=wgbuf[_b].rearrange("p (c f) -> p c f", f=64)[:, :, :],
                in_ap=whbm[:, :],
                idxs_ap=idx16[:, (_e - 1) * (CAP // 16):_e * (CAP // 16)],
                num_idxs=CAP, num_idxs_reg=regcell[_i], elem_size=64,
                transpose=False),
                dma=True, after=[t_whw, wg_lastuse[buf], t_mz2])
            t_gathe[e_next] = (t_ga, t_gw2, buf, gbuf)

        def _e_r(e):
            return e - 1

        scatter_done = [None, None]

        def flush_scatter():
            nonlocal t_scatter
            if pend_scatter[0] is None:
                return
            _e, _yev = pend_scatter[0]
            pend_scatter[0] = None
            t_scatter = b.emit("gpsimd", lambda e, _ee=_e: e.dma_scatter_add(
                out_ap=dy[:, :],
                in_ap=ye_bufs[_e % 2].rearrange("p (c f) -> p c f", f=D)[:, :, :],
                idxs_ap=idx16[:, (_ee - 1) * (CAP // 16):_ee * (CAP // 16)],
                num_idxs=CAP, num_idxs_reg=regcell[_ee % 2], elem_size=D),
                dma=True,
                after=[t_store, t_scatter, _yev, t_repl])
            scatter_done[_e % 2] = t_scatter

        for e_i in range(1 + N_ROUTED):
            routed = e_i >= 1
            nth = 1 if routed else NTH
            t_wd_load = None
            if routed:
                t_ga, t_gw2, wbuf, gbuf_e = t_gathe[e_i]
            for th in range(nth):
                for dck in range(NDF):
                    buf = dck % 2
                    ring_tick[("g", buf)] = b.emit("gpsimd",
                        lambda e, _e=e_i, _d=dck, _b=buf: e.dma_start(
                            out=wgq[_b][:],
                            in_=dw[f"wg{_e}"][:, _d * 128:(_d + 1) * 128]
                                .rearrange("(c p) f -> p c f", p=128)),
                        dma=True, after=[ring_lastuse.get(("g", buf))] + attn_done)
                    ring_tick[("u", buf)] = b.emit("gpsimd",
                        lambda e, _e=e_i, _d=dck, _b=buf: e.dma_start(
                            out=wuq[_b][:],
                            in_=dw[f"wu{_e}"][:, _d * 128:(_d + 1) * 128]
                                .rearrange("(c p) f -> p c f", p=128)),
                        dma=True, after=[ring_lastuse.get(("u", buf))] + attn_done)
                    if t_wd_load is None:
                        t_wd_load = b.emit("gpsimd", lambda e, _e=e_i: e.dma_start(
                            out=wdb[:], in_=dw[f"wd{_e}"].rearrange(
                                "(c p) f -> p c f", p=128)),
                            dma=True, after=[t_wd_lastuse] + attn_done)
                    par = dck % 2
                    gbi = (0, 1) if par == 0 else (4, 5)
                    ubi = (2, 3) if par == 0 else (6, 7)
                    g_t, u_t = [], []
                    for s in range(2):
                        for kind, bis, wq_ in (("g", gbi, wgq[buf]),
                                               ("u", ubi, wuq[buf])):
                            emits = []
                            for dc in range(ND):
                                if routed:
                                    emits.append(lambda e, bk, st, sp, _dc=dc,
                                                 _s=s, _w=wq_, _g=gbuf_e: e.matmul(
                                        bk[:],
                                        _w[:, _dc * 128:(_dc + 1) * 128],
                                        _g[:, _dc * CAP + _s * 512:
                                           _dc * CAP + (_s + 1) * 512],
                                        start=st, stop=sp))
                                else:
                                    emits.append(lambda e, bk, st, sp, _dc=dc,
                                                 _s=s, _th=th, _w=wq_: e.matmul(
                                        bk[:],
                                        _w[:, _dc * 128:(_dc + 1) * 128],
                                        ln2T[:, _dc * T + _th * TH + _s * 512:
                                             _dc * T + _th * TH + (_s + 1) * 512],
                                        start=st, stop=sp))
                            aft = [ring_tick[(kind, buf)], ln2_done,
                                   gate_ticks[-1]]
                            if routed:
                                aft.append(t_ga)
                            gt, _ = mm_group(bis[s], emits, extra_after=aft)
                            (g_t if kind == "g" else u_t).append(gt)
                    for s in range(2):
                        t_sg = b.emit("scalar", lambda e, _s=s, _bk=pb[gbi[s]]: e.activation(
                            out=sgb[:, _s * 512:(_s + 1) * 512], in_=_bk[:],
                            func=AF.Silu, bias=0.0, scale=1.0),
                            after=[g_t[s], pc.get("sgb%d" % s)])
                        bank_last[gbi[s]] = t_sg
                        t_h = b.emit("vector", lambda e, _s=s, _dck=dck,
                                     _bk=pb[ubi[s]]: e.tensor_tensor(
                            out=hid[:, _dck * TH + _s * 512:_dck * TH + (_s + 1) * 512],
                            in0=sgb[:, _s * 512:(_s + 1) * 512], in1=_bk[:],
                            op=ALU.mult),
                            after=[u_t[s], t_sg, t_hid_lastuse])
                        bank_last[ubi[s]] = t_h
                        pc["sgb%d" % s] = t_h
                        hid_ticks[(dck, s)] = t_h
                    ring_lastuse[("g", buf)] = g_t[-1]
                    ring_lastuse[("u", buf)] = u_t[-1]
                a_done_hist[e_i] = u_t[-1]
                flush_scatter()
                if th == nth - 1 and e_i < N_ROUTED:
                    emit_gathers(e_i + 1, a_done_hist.get(e_i - 1))
                down_last = []
                ye_evs = []
                for tc in range(8):
                    gc = th * 8 + tc
                    bi = (0, 1) if tc % 2 == 0 else (2, 3)
                    for s in range(2):
                        emits = []
                        for dck in range(NDF):
                            emits.append(lambda e, bk, st, sp, _dck=dck, _tc=tc,
                                         _s=s: e.matmul(
                                bk[:],
                                hid[:, _dck * TH + _tc * 128:_dck * TH + (_tc + 1) * 128],
                                wdb[:, _dck * D + _s * 512:_dck * D + (_s + 1) * 512],
                                start=st, stop=sp))
                        hid_all = [hid_ticks[(d2, s2)] for d2 in range(NDF)
                                   for s2 in range(2)] if (tc == 0 and s == 0) else []
                        if not routed:
                            xslice = xres[:, gc * D + s * 512:gc * D + (s + 1) * 512]
                            evac = lambda e, bk, _x=xslice: e.tensor_tensor(
                                out=_x, in0=bk[:], in1=_x, op=ALU.add)
                            eg = []
                        else:
                            yeb = ye_bufs[e_i % 2]
                            ysl = yeb[:, tc * D + s * 512:tc * D + (s + 1) * 512]
                            evac = lambda e, bk, _y=ysl, _tc=tc, _e=e_i, _b=wbuf:                                 e.tensor_scalar(
                                    out=_y, in0=bk[:],
                                    scalar1=wgbuf[_b][:, _tc * 64 + (_e - 1):
                                                      _tc * 64 + _e],
                                    scalar2=None, op0=ALU.mult)
                            eg = [t_gw2, scatter_done[e_i % 2], t_store]
                        gt, ev = mm_group(bi[s], emits,
                                          extra_after=[t_wd_load] + hid_all,
                                          evac=evac, evac_eng="vector",
                                          evac_guard=eg)
                        down_last.append(gt)
                        ye_evs.append(ev)
                        last_ev = ev
                t_hid_lastuse = down_last[-1]
                if routed:
                    wg_lastuse[wbuf] = ye_evs[-1]
            t_wd_lastuse = down_last[-1]
            if not routed:
                t_store = b.emit("sync", lambda e: e.dma_start(
                    out=dy.rearrange("(c p) f -> p c f", p=128), in_=xres[:]),
                    dma=True, after=[last_ev])
            else:
                pend_scatter[0] = (e_i, ye_evs[-1])
                if e_i == N_ROUTED:
                    flush_scatter()

        def final_wait(e, _f=t_scatter, _g=t_store):
            e.wait_ge(b.sems[_g[0]], _g[1])
            if _f is not None:
                e.wait_ge(b.sems[_f[0]], _f[1])
        b.ev["sync"].append(final_wait)

        block = es.enter_context(nc.Block())

        @block.tensor
        def _(t):
            for fn in b.ev["tensor"]:
                fn(t)

        @block.vector
        def _(v):
            for fn in b.ev["vector"]:
                fn(v)

        @block.scalar
        def _(s):
            for fn in b.ev["scalar"]:
                fn(s)

        @block.gpsimd
        def _(g):
            for fn in b.ev["gpsimd"]:
                fn(g)

        @block.sync
        def _(sy):
            for fn in b.ev["sync"]:
                fn(sy)

    import bass_rust as _bass_rust
    from concourse.library_config import all_libraries, standard
    _mask = {}
    for _lib in all_libraries:
        for _it in _lib.instructions:
            _mask[_it] = _mask.get(_it, 0) | (1 << _lib.index)
    _bass_rust.insert_library_loads(nc, _mask, len(all_libraries), standard.index)
    mybir.codegen_inst_isa_subclasses(nc)
    return nc


def _in_map(inputs, bi):
    g = np.ascontiguousarray
    m = {
        "x": inputs["L0_attn_hidden_states"][bi],
        "wq": inputs["L0_attn_Wq"], "wk": inputs["L0_attn_Wk"],
        "wv": inputs["L0_attn_Wv"], "wqa": inputs["L0_attn_Wqa"],
        "wka": inputs["L0_attn_Wka"],
        "bq": inputs["L0_attn_bq"].reshape(1, D),
        "bk": inputs["L0_attn_bk"].reshape(1, D),
        "bv": inputs["L0_attn_bv"].reshape(1, D),
        "bqa": inputs["L0_attn_bqa"].reshape(1, D),
        "bka": inputs["L0_attn_bka"].reshape(1, D),
        "gcol": inputs["L0_attn_gamma"].reshape(ND, 128).T,
        "bcol": inputs["L0_attn_beta"].reshape(ND, 128).T,
        "gatew": inputs["L0_ffn_gate_w"],
        "wg0": inputs["L0_ffn_ws_gate"], "wu0": inputs["L0_ffn_ws_up"],
        "wd0": inputs["L0_ffn_ws_down"],
    }
    for e in range(8):
        m[f"wg{e + 1}"] = inputs["L0_ffn_w_gate"][e]
        m[f"wu{e + 1}"] = inputs["L0_ffn_w_up"][e]
        m[f"wd{e + 1}"] = inputs["L0_ffn_w_down"][e]
    return {k: g(np.asarray(v, dtype=np.float32)) for k, v in m.items()}


def kernel(**inputs):
    nc = build_program()
    in_maps = [_in_map(inputs, bi) for bi in range(8)]
    res = run_bass_kernel_spmd(nc, in_maps, core_ids=list(range(8)))
    out = np.stack([res.results[i]["y"] for i in range(8)], axis=0)
    return np.ascontiguousarray(out.astype(np.float32))
